# version 06
# baseline (speedup 1.0000x reference)
import contextlib

import numpy as np

import concourse.bass as bass
import concourse.mybir as mybir
from concourse.bass_utils import run_bass_kernel_spmd

F32 = mybir.dt.float32
BF16 = mybir.dt.bfloat16
I32 = mybir.dt.int32
AL = mybir.AluOpType
AF = mybir.ActivationFunctionType
AXX = mybir.AxisListType.X

D, DC = 2048, 16
R, RC = 2048, 16
DH, HC = 512, 4
E, BOT, BC = 8, 512, 4
TM, TP = 1024, 3072
EPS = 1e-5

B_BU, B_BV, B_BG2, B_BG1, B_CB, B_BSD, B_BSU, B_BD, B_CW = (
    0, 16, 32, 48, 52, 68, 72, 88, 120)
NBR = 168
ARENA_COLS = 77400


class Arena:
    def __init__(self, t):
        self.t = t
        self.live = {}

    def alloc(self, name, cols):
        cols = (cols + 511) // 512 * 512
        off = 0
        for s, e_ in sorted(self.live.values()):
            if off + cols <= s:
                break
            off = max(off, e_)
        assert off + cols <= ARENA_COLS, (
            f"arena overflow: {name} {cols} at {off}; live={self.live}")
        self.live[name] = (off, off + cols)

    def free(self, *names):
        for n in names:
            del self.live[n]

    def rename(self, old, new):
        self.live[new] = self.live.pop(old)

    def ap(self, name):
        s, e_ = self.live[name]
        return self.t[:, s:e_]

    def v3(self, name, nch, w):
        s, e_ = self.live[name]
        assert nch * w <= e_ - s, (name, nch, w, e_ - s)
        return self.t[:, s:s + nch * w].rearrange("p (c t) -> p c t", t=w)

    def f32(self, name):
        s, e_ = self.live[name]
        return self.t[:, s:e_].bitcast(F32)


def build_program(nonce0, nonce1, debug=False):
    nc = bass.Bass("TRN2", target_bir_lowering=False, debug=False,
                   detect_race_conditions=False)

    din = lambda n, s, d=F32: nc.dram_tensor(n, s, d, kind="ExternalInput").ap()
    xm_d = din("xm", [128, DC, TM + 2])
    xp_d = din("xp", [6, 128, DC * 512])
    wuT_d = din("wuT", [D, R])
    wvT_d = din("wvT", [D, R])
    wg1T_d = din("wg1T", [D, DH])
    wg2T_d = din("wg2T", [DH, R])
    gateT_d = din("gateT", [D, E])
    wdT_d = din("wdT", [E + 1, D, BOT])
    wuPM_d = din("wuPM", [DC, 128, 36 * 128])
    buR_d = din("buR", [DC, E, 128])
    brows_d = din("brows", [NBR, 128])
    npcols_d = din("npcols", [128, 2])
    oneh_d = din("onehots", [8, E * 128])
    y_d = nc.dram_tensor("y", [D, TM], F32, kind="ExternalOutput").ap()

    dsc = lambda n, s, d=BF16: nc.dram_tensor(n, s, d, kind="Internal",
                                              addr_space="Local").ap()
    xn_sp = dsc("xn_sp", [8, 128, DC * 512])
    gt_sp = dsc("gt_sp", [8, 128, DC * 512])
    cv_sp = dsc("cv_sp", [2, 128, DC * 512])
    kc_sp = dsc("kc_sp", [RC, 2, 128, 512], F32)
    x1_sp = dsc("x1_sp", [128, DC * 1024])
    g_sp = dsc("g_sp", [128, RC * 512])
    wub_sp = dsc("wub_sp", [D, R])
    xch = nc.dram_tensor("xch", [2, 2, 128, 16], F32, kind="Internal",
                         addr_space="Shared").ap()
    xfl = nc.dram_tensor("xfl", [2, 2], I32, kind="Internal",
                         addr_space="Shared").ap()

    dbg = {}
    if debug:
        for nm, shp in (("off", [128, 16]), ("srun", [128, 16]),
                        ("ccol", [128, 16]), ("comb", [E, TM])):
            dbg[nm] = nc.dram_tensor("dbg_" + nm, shp, F32,
                                     kind="ExternalOutput").ap()
        for nm, src_shape, dt_ in (("wub", [D, R], BF16),
                                   ("gsp", [128, RC * 512], BF16),
                                   ("atn", [128, DC * 1024], BF16),
                                   ("xn", [8, 128, DC * 512], BF16),
                                   ("gt", [8, 128, DC * 512], BF16),
                                   ("kc", [RC, 2, 128, 512], F32),
                                   ("cv", [2, 128, DC * 512], BF16),
                                   ("x1", [128, DC * 1024], BF16)):
            dbg[nm] = nc.dram_tensor("dbg_" + nm, src_shape, dt_,
                                     kind="ExternalOutput").ap()

    es = contextlib.ExitStack()
    SB = lambda n, s, d=BF16: es.enter_context(nc.sbuf_tensor(n, s, d))
    PSA = lambda n, s, d=F32: es.enter_context(nc.psum_tensor(n, s, d))
    SEM = lambda n: es.enter_context(nc.semaphore(n))

    idf = SB("idf", [128, 128], F32)
    iota_t = SB("iota_t", [128, 128], I32)
    ones_col = SB("ones_col", [128, 1], BF16)
    ones_row = SB("ones_row", [1, 128], BF16)
    brows1 = SB("brows1", [128, 128], F32)
    brows2 = SB("brows2", [NBR - 128, 128], F32)
    bcols = SB("bcols", [128, NBR], F32)
    ccol = SB("ccol", [128, 16], F32)
    offc = SB("offc", [128, 16], F32)
    sza = SB("sza", [128, 64], F32)
    srun = SB("srun", [128, 16], F32)
    kl_col = SB("kl_col", [128, 16], F32)
    acol = SB("acol", [128, 1], F32)
    acols = SB("acols", [128, 96], F32)
    npz = SB("npz", [128, 2], F32)
    eps_col = SB("eps_col", [1, 1], F32)
    nonce_t = SB("nonce_t", [1, 2], I32)
    flag_t = SB("flag_t", [1, 2], I32)
    strow = SB("strow", [1, 6 * 512], F32)
    ab_bf = SB("ab_bf", [1, 2 * 512], BF16)
    ab2_bf = SB("ab2_bf", [1, 2 * 512], BF16)
    ABt = SB("ABt", [128, 2 * 512], BF16)
    hb0 = SB("hb0", [128, 4], BF16)
    cgb = SB("cgb", [128, 16], BF16)
    hh = SB("hh", [128, HC, 512], BF16)
    ugv = SB("ugv", [128, 512], F32)
    kcsb = [SB("kcs%d" % i, [128, 512], F32) for i in range(2)]
    gelb = [SB("gel%d" % i, [128, 512], BF16) for i in range(2)]
    lgr = SB("lgr", [8, 1024], F32)
    cmr = SB("cmr", [8, 1024], F32)
    cmb = SB("cmb", [8, 1024], BF16)
    lgm = SB("lgm", [128, 8, 8], F32)
    cmt = SB("cmt", [128, 8, 8], F32)
    qxp = SB("qxp", [128, 8, 8], F32)
    m1a = SB("m1a", [128, 8], F32)
    m2a = SB("m2a", [128, 8], F32)
    e2a = SB("e2a", [128, 8], F32)
    gsc = SB("gsc", [128, 8], F32)
    buRb = SB("buRb", [8, DC * 128], BF16)
    oneh = SB("oneh", [8, E * 128], BF16)
    outb = [SB("outb%d" % i, [128, 512], F32) for i in range(2)]

    arena_t = SB("arena", [128, ARENA_COLS], BF16)
    A = Arena(arena_t)

    PM = [PSA("PM%d" % i, [128, 512]) for i in range(4)]
    PBb = [PSA("PBb%d" % i, [128, 512]) for i in range(2)]
    PR = [PSA("PR%d" % i, [1, 512]) for i in range(2)]

    PH = SEM("PH"); G_ = SEM("G_"); SD = SEM("SD"); TMS = SEM("TMS")
    VE = SEM("VE"); SE = SEM("SE"); FB = SEM("FB"); PXs = SEM("PXs")

    c = {"g": 0, "s": 0, "t": 0, "v": 0, "sc": 0, "fb": 0, "ph": 0}

    def bump(k, n=1):
        c[k] += n
        return c[k]

    with es, nc.Block() as block:

        def barrier():
            c["ph"] += 5
            tgt = c["ph"]
            sv, gv = c["s"] * 16, c["g"] * 16
            @block.sync
            def _(sy):
                sy.wait_ge(SD, sv)
                sy.sem_inc(PH, 1)
                sy.wait_ge(PH, tgt)
            @block.gpsimd
            def _(gp):
                gp.wait_ge(G_, gv)
                gp.sem_inc(PH, 1)
                gp.wait_ge(PH, tgt)
            tv, vv, sv2 = c["t"], c["v"], c["sc"]
            @block.tensor
            def _(t):
                t.wait_ge(TMS, tv)
                t.sem_inc(PH, 1)
                t.wait_ge(PH, tgt)
            @block.vector
            def _(v):
                v.wait_ge(VE, vv)
                v.sem_inc(PH, 1)
                v.wait_ge(PH, tgt)
            @block.scalar
            def _(s):
                s.wait_ge(SE, sv2)
                s.sem_inc(PH, 1)
                s.wait_ge(PH, tgt)

        def r3(dram2, a, b):
            return dram2[:, a:b].rearrange("(c p) t -> p c t", p=128)

        def pipe(n, t_emit, e_eng, e_emit, e_incs=1, depth=2):
            key = {"vector": "v", "scalar": "sc"}[e_eng]
            sem = {"vector": VE, "scalar": SE}[e_eng]
            t0, e0 = c["t"], c[key]
            @block.tensor
            def _(t):
                for i in range(n):
                    if i >= depth:
                        t.wait_ge(sem, e0 + (i - depth + 1) * e_incs)
                    t_emit(t, i)
            bump("t", n)
            sec = block.vector if e_eng == "vector" else block.scalar
            @sec
            def _(e):
                for i in range(n):
                    e.wait_ge(TMS, t0 + i + 1)
                    e_emit(e, i)
            bump(key, n * e_incs)
            return t0, e0

        A.alloc("wg1", DC * 512)
        A.alloc("wg2", HC * 2048)

        @block.gpsimd
        def _(gp):
            gp.iota(iota_t[:], [[1, 128]], channel_multiplier=-1).then_inc(FB, 1)
            gp.dma_start(out=oneh[:], in_=oneh_d[:]).then_inc(G_, 16)
            gp.dma_start(out=A.v3("wg1", DC, 512),
                         in_=r3(wg1T_d, 0, DH)).then_inc(G_, 16)
            gp.dma_start(out=A.v3("wg2", HC, 2048),
                         in_=wg2T_d.rearrange("(c p) r -> p c r", p=128)).then_inc(G_, 16)
        bump("fb")
        g_wg = bump("g", 3) * 16

        @block.sync
        def _(sy):
            sy.dma_start(out=brows1[:], in_=brows_d[0:128, :]).then_inc(SD, 16)
            sy.dma_start(out=brows2[:], in_=brows_d[128:NBR, :]).then_inc(SD, 16)
            sy.dma_start(out=npz[:], in_=npcols_d[:]).then_inc(SD, 16)
        s_brows = bump("s", 3) * 16

        @block.vector
        def _(v):
            v.memset(ones_col[:], 1.0)
            v.memset(ones_row[:], 1.0)
            v.memset(srun[:], 0.0)
            v.memset(kl_col[:], 0.0)
            v.memset(nonce_t[0:1, 0:1], nonce0)
            v.memset(nonce_t[0:1, 1:2], nonce1)
            v.memset(eps_col[:], float(EPS))
            v.wait_ge(FB, 1)
            v.tensor_single_scalar(idf[:], iota_t[:], 0, AL.is_equal).then_inc(VE, 1)
        v_idf = bump("v")

        @block.tensor
        def _(t):
            t.wait_ge(VE, v_idf)
            t.wait_ge(SD, s_brows)
            t.transpose(PM[0][:, 0:128], brows1[:], idf[0:128, 0:128])
            t.transpose(PM[1][:, 0:NBR - 128], brows2[:],
                        idf[0:NBR - 128, 0:NBR - 128]).then_inc(TMS, 1)
        t_bias = bump("t")

        @block.scalar
        def _(s):
            s.wait_ge(TMS, t_bias)
            s.copy(bcols[:, 0:128], PM[0][:, 0:128])
            s.copy(bcols[:, 128:NBR], PM[1][:, 0:NBR - 128])
            s.activation(hb0[:], bcols[:, B_BG1:B_BG1 + 4], AF.Relu).then_inc(SE, 1)
        sc_hb0 = bump("sc")

        @block.tensor
        def _(t):
            t.wait_ge(SE, sc_hb0)
            t.wait_ge(G_, g_wg)
            wg2 = A.v3("wg2", HC, 2048)
            for r in range(RC):
                for h in range(HC):
                    i = t.matmul(PM[2][:, r:r + 1], wg2[:, h, r * 128:(r + 1) * 128],
                                 hb0[:, h:h + 1], start=(h == 0), stop=(h == HC - 1))
            i.then_inc(TMS, 1)
        t_cg = bump("t")

        @block.scalar
        def _(s):
            s.wait_ge(TMS, t_cg)
            for r in range(RC):
                i = s.activation(cgb[:, r:r + 1], PM[2][:, r:r + 1], AF.Sigmoid,
                                 bias=bcols[:, B_BG2 + r:B_BG2 + r + 1])
            i.then_inc(SE, 1)
        sc_cg = bump("sc")

        @block.vector
        def _(v):
            v.wait_ge(SE, sc_cg)
            v.tensor_mul(ccol[:], cgb[:], bcols[:, B_BU:B_BU + 16]).then_inc(VE, 1)
        bump("v")

        barrier()

        A.alloc("xnm", DC * 1026)
        A.alloc("gth", DC * 512)
        A.alloc("xin0", DC * 513)
        A.alloc("xin1", DC * 513)
        A.alloc("sqb0", DC * 512)
        A.alloc("sqb1", DC * 512)

        g_xp = []
        fb_p1 = c["fb"]
        @block.gpsimd
        def _(gp):
            for k in range(6):
                if k >= 2:
                    gp.wait_ge(FB, fb_p1 + k - 1)
                gp.dma_start(out=A.ap("xin%d" % (k % 2))[:, 0:DC * 512],
                             in_=xp_d[k]).then_inc(G_, 16)
                g_xp.append(bump("g") * 16)
            gp.wait_ge(FB, fb_p1 + 6)
            gp.dma_start(out=A.v3("xin0", DC, 513),
                         in_=xm_d[:, :, 0:513]).then_inc(G_, 16)
            g_xp.append(bump("g") * 16)
            gp.dma_start(out=A.v3("xin1", DC, 513),
                         in_=xm_d[:, :, 513:1026]).then_inc(G_, 16)
            g_xp.append(bump("g") * 16)

        ln_state = {"k": 0, "norm": {}, "v1": {}}

        def emit_ln(src_of, W, g_need, dst_of):
            k = ln_state["k"]
            ln_state["k"] = k + 1
            sq3 = lambda: A.v3("sqb%d" % (k % 2), DC, 512)
            @block.scalar
            def _(s):
                s.wait_ge(G_, g_need)
                if k >= 2:
                    s.wait_ge(VE, ln_state["norm"][k - 2])
                sq = sq3()
                for d in range(DC):
                    i = s.activation(sq[:, d, 0:W], src_of(d), AF.Square)
                i.then_inc(SE, 1)
            s_sq = bump("sc")
            @block.tensor
            def _(t):
                t.wait_ge(SE, s_sq)
                if k >= 1:
                    t.wait_ge(VE, ln_state["v1"][k - 1])
                sq = sq3()
                for d in range(DC):
                    t.matmul(PR[0][0:1, 0:W], ones_col[:], src_of(d),
                             start=(d == 0), stop=(d == DC - 1))
                for d in range(DC):
                    i = t.matmul(PR[1][0:1, 0:W], ones_col[:], sq[:, d, 0:W],
                                 start=(d == 0), stop=(d == DC - 1))
                i.then_inc(TMS, 1)
            t_st = bump("t")
            @block.vector
            def _(v):
                v.wait_ge(TMS, t_st)
                m, ms = strow[0:1, 0:W], strow[0:1, 512:512 + W]
                var = strow[0:1, 1024:1024 + W]
                v.tensor_single_scalar(m, PR[0][0:1, 0:W], 1.0 / D, AL.mult)
                v.tensor_single_scalar(ms, PR[1][0:1, 0:W], 1.0 / D, AL.mult)
                v.tensor_tensor(var, m, m, AL.mult)
                v.tensor_sub(var, ms, var).then_inc(VE, 1)
            v1 = bump("v")
            ln_state["v1"][k] = v1
            @block.scalar
            def _(s):
                s.wait_ge(VE, v1)
                s.activation(strow[0:1, 1536:1536 + W], strow[0:1, 1024:1024 + W],
                             AF.Sqrt, bias=eps_col[0:1, 0:1]).then_inc(SE, 1)
            s1 = bump("sc")
            @block.vector
            def _(v):
                v.wait_ge(SE, s1)
                a, b = strow[0:1, 2048:2048 + W], strow[0:1, 2560:2560 + W]
                v.reciprocal(a, strow[0:1, 1536:1536 + W])
                v.scalar_tensor_tensor(b, strow[0:1, 0:W], -1.0, a, AL.mult, AL.mult)
                v.tensor_copy(ab_bf[0:1, 0:W], a)
                v.tensor_copy(ab_bf[0:1, 512:512 + W], b).then_inc(VE, 1)
            v2 = bump("v")
            @block.tensor
            def _(t):
                t.wait_ge(VE, v2)
                t.matmul(PBb[0][:, 0:W], ones_row[:], ab_bf[0:1, 0:W],
                         start=True, stop=True)
                t.matmul(PBb[1][:, 0:W], ones_row[:], ab_bf[0:1, 512:512 + W],
                         start=True, stop=True).then_inc(TMS, 1)
            t2 = bump("t")
            @block.scalar
            def _(s):
                s.wait_ge(TMS, t2)
                s.copy(ABt[:, 0:W], PBb[0][:, 0:W])
                s.copy(ABt[:, 512:512 + W], PBb[1][:, 0:W]).then_inc(SE, 1)
            s2 = bump("sc")
            @block.vector
            def _(v):
                v.wait_ge(SE, s2)
                sq = sq3()
                for d in range(DC):
                    v.tensor_mul(sq[:, d, 0:W], src_of(d), ABt[:, 0:W])
                for d in range(DC):
                    i = v.tensor_add(dst_of(d), sq[:, d, 0:W], ABt[:, 512:512 + W])
                i.then_inc(VE, 1)
            nd = bump("v")
            ln_state["norm"][k] = nd
            return nd

        gsv = {"h_ev": [0, 0], "g_ev": [0, 0], "g2_last": 0, "sp": 0}

        def emit_gates(xn_of, spill_col0, v_need):
            h_mm, g_mm = [0] * HC, [0] * RC
            scp = c["sc"]
            @block.tensor
            def _(t):
                t.wait_ge(VE, v_need)
                t.wait_ge(SE, scp)
                wg1 = A.v3("wg1", DC, 512)
                for h in range(HC):
                    if h >= 2:
                        t.wait_ge(SE, scp + h - 1)
                    for d in range(DC):
                        i = t.matmul(PM[h % 2][:, 0:512],
                                     wg1[:, d, h * 128:(h + 1) * 128],
                                     xn_of(d), start=(d == 0), stop=(d == DC - 1))
                    i.then_inc(TMS, 1)
                    h_mm[h] = bump("t")
            @block.scalar
            def _(s):
                s.wait_ge(TMS, gsv["g2_last"])
                for h in range(HC):
                    s.wait_ge(TMS, h_mm[h])
                    s.activation(hh[:, h, 0:512], PM[h % 2][:, 0:512], AF.Relu,
                                 bias=bcols[:, B_BG1 + h:B_BG1 + h + 1]).then_inc(SE, 1)
            bump("sc", HC)
            h_all = c["sc"]
            @block.tensor
            def _(t):
                wg2 = A.v3("wg2", HC, 2048)
                for r in range(RC):
                    if r < 2:
                        t.wait_ge(SE, h_all)
                    else:
                        t.wait_ge(SE, h_all + r - 1)
                    for h in range(HC):
                        i = t.matmul(PM[2 + r % 2][:, 0:512],
                                     wg2[:, h, r * 128:(r + 1) * 128],
                                     hh[:, h, 0:512], start=(h == 0), stop=(h == HC - 1))
                    i.then_inc(TMS, 1)
                    g_mm[r] = bump("t")
            @block.scalar
            def _(s):
                gth = A.v3("gth", DC, 512)
                s.wait_ge(SD, gsv["sp"])
                for r in range(RC):
                    s.wait_ge(TMS, g_mm[r])
                    s.activation(gth[:, r, 0:512], PM[2 + r % 2][:, 0:512], AF.Sigmoid,
                                 bias=bcols[:, B_BG2 + r:B_BG2 + r + 1]).then_inc(SE, 1)
            bump("sc", RC)
            g_all = c["sc"]
            @block.sync
            def _(sy):
                sy.wait_ge(SE, g_all)
                sy.dma_start(out=gt_sp[spill_col0],
                             in_=A.ap("gth")[:, 0:DC * 512]).then_inc(SD, 16)
            bump("s")
            gsv["g2_last"] = g_mm[RC - 1]
            gsv["sp"] = c["s"] * 16
            return g_mm[RC - 1], c["s"] * 16

        for k in range(6):
            buf = "xin%d" % (k % 2)
            src = lambda d, b=buf: A.v3(b, DC, 512)[:, d, 0:512]
            xn_done = emit_ln(src, 512, g_xp[k], src)
            @block.sync
            def _(sy):
                sy.wait_ge(VE, xn_done)
                sy.dma_start(out=xn_sp[k],
                             in_=A.ap(buf)[:, 0:DC * 512]).then_inc(SD, 16)
            s_xn = bump("s") * 16
            g_last, s_gt = emit_gates(src, k, xn_done)
            @block.vector
            def _(v, xn_done=xn_done):
                v.wait_ge(TMS, g_last)
                v.wait_ge(SD, s_xn)
                v.wait_ge(VE, xn_done)
                v.sem_inc(FB, 1)
            bump("fb")

        xnm3 = lambda: A.v3("xnm", DC, 1026)
        for (cs, cw_, gi, bufname, boff) in ((0, 512, 6, "xin0", 0),
                                             (1, 512, 6, "xin0", 1),
                                             (513, 512, 7, "xin1", 0),
                                             (514, 512, 7, "xin1", 1)):
            v_xnm = emit_ln(
                lambda d, b=bufname, o=boff, w=cw_: A.v3(b, DC, 513)[:, d, o:o + w],
                cw_, g_xp[gi],
                lambda d, o=cs, w=cw_: xnm3()[:, d, o:o + w])

        @block.sync
        def _(sy):
            sy.wait_ge(VE, v_xnm)
            sy.dma_start(out=xn_sp[6],
                         in_=xnm3()[:, :, 1:513]).then_inc(SD, 16)
            sy.dma_start(out=xn_sp[7],
                         in_=xnm3()[:, :, 513:1025]).then_inc(SD, 16)
        bump("s", 2)

        A.free("sqb0", "sqb1", "xin0", "xin1")
        A.alloc("cvb", DC * 1024)
        A.alloc("cvf", 2 * 1024)
        sc_before_conv = c["sc"]
        @block.vector
        def _(v):
            v.wait_ge(VE, v_xnm)
            v.wait_ge(SE, sc_before_conv)
            cvb = A.v3("cvb", DC, 1024)
            cvf = A.f32("cvf")[:, 0:1024]
            x3 = xnm3()
            for d in range(DC):
                v.tensor_scalar(cvf, x3[:, d, 0:1024],
                                bcols[:, B_CW + d:B_CW + d + 1],
                                bcols[:, B_CB + d:B_CB + d + 1], AL.mult, AL.add)
                v.scalar_tensor_tensor(cvf, x3[:, d, 1:1025],
                                       bcols[:, B_CW + 16 + d:B_CW + 17 + d],
                                       cvf, AL.mult, AL.add)
                i = v.scalar_tensor_tensor(cvb[:, d, 0:1024], x3[:, d, 2:1026],
                                           bcols[:, B_CW + 32 + d:B_CW + 33 + d],
                                           cvf, AL.mult, AL.add)
            i.then_inc(VE, 1)
        v_conv = bump("v")

        s_cvsp = (c["s"] + 1) * 16
        @block.sync
        def _(sy):
            sy.wait_ge(VE, v_conv)
            sy.dma_start(out=cv_sp[0], in_=A.v3("cvb", DC, 1024)[:, :, 0:512]).then_inc(SD, 16)
            sy.dma_start(out=cv_sp[1], in_=A.v3("cvb", DC, 1024)[:, :, 512:1024]).then_inc(SD, 16)
        bump("s", 2)

        A.free("cvf", "cvb")
        fb_wu = c["fb"]
        @block.vector
        def _(v):
            v.wait_ge(SD, s_cvsp)
            v.wait_ge(VE, v_conv)
            v.sem_inc(FB, 1)
        bump("fb")
        A.alloc("wu", DC * 2048)
        @block.gpsimd
        def _(gp):
            gp.wait_ge(FB, fb_wu + 1)
            gp.dma_start(out=A.v3("wu", DC, 2048), in_=r3(wuT_d, 0, R)).then_inc(G_, 16)
        g_wu = bump("g") * 16

        for j in range(2):
            emit_gates(lambda d, j=j: xnm3()[:, d, 1 + j * 512:513 + j * 512],
                       6 + j, v_xnm)

        barrier()
        A.free("xnm", "gth", "wg1", "wg2")

        A.alloc("xrc0", DC * 512)
        A.alloc("xrc1", DC * 512)
        A.alloc("gtc0", DC * 512)
        A.alloc("gtc1", DC * 512)
        A.alloc("kcc", 2 * 1024)

        @block.sync
        def _(sy):
            sy.wait_ge(G_, g_wu)
            sy.dma_start(out=wub_sp.rearrange("(c p) r -> p c r", p=128),
                         in_=A.v3("wu", DC, 2048)).then_inc(SD, 16)
        bump("s")

        vend = {}
        for hidx, (kind, j) in enumerate([("p", k) for k in range(6)] + [("m", 0), ("m", 1)]):
            xa = j if kind == "p" else 6 + j
            pv = vend.get(hidx - 2, 0)
            xrn, gtn = "xrc%d" % (hidx % 2), "gtc%d" % (hidx % 2)
            @block.sync
            def _(sy, xa=xa, pv=pv, xrn=xrn, gtn=gtn):
                sy.wait_ge(VE, pv)
                sy.dma_start(out=A.ap(xrn)[:, 0:DC * 512],
                             in_=xn_sp[xa]).then_inc(SD, 16)
                sy.dma_start(out=A.ap(gtn)[:, 0:DC * 512],
                             in_=gt_sp[xa]).then_inc(SD, 16)
            s_ld = bump("s", 2) * 16

            def t_emit(t, i, s_ld=s_ld, xrn=xrn):
                if i == 0:
                    t.wait_ge(SD, s_ld)
                wu = A.v3("wu", DC, 2048)
                xr = A.v3(xrn, DC, 512)
                for d in range(DC):
                    mm = t.matmul(PM[i % 4][:, 0:512], wu[:, d, i * 128:(i + 1) * 128],
                                  xr[:, d, 0:512], start=(d == 0), stop=(d == DC - 1))
                mm.then_inc(TMS, 1)

            if kind == "p":
                def v_emit(v, i, hidx=hidx, gtn=gtn):
                    gt = A.v3(gtn, DC, 512)
                    v.scalar_tensor_tensor(ugv[:], PM[i % 4][:, 0:512],
                                           bcols[:, B_BU + i:B_BU + i + 1],
                                           gt[:, i, 0:512], AL.add, AL.mult)
                    v.reduce_sum(acols[:, hidx * 16 + i:hidx * 16 + i + 1],
                                 ugv[:], AXX).then_inc(VE, 1)
                pipe(RC, t_emit, "vector", v_emit, depth=4)
            else:
                s0_kc = c["s"]
                v0 = c["v"]
                def v_emit(v, i, s0_kc=s0_kc, v0=v0, gtn=gtn):
                    gt = A.v3(gtn, DC, 512)
                    v.wait_ge(SD, (s0_kc + i - 1) * 16 if i >= 2 else s0_kc * 16)
                    v.scalar_tensor_tensor(ugv[:], PM[i % 4][:, 0:512],
                                           bcols[:, B_BU + i:B_BU + i + 1],
                                           gt[:, i, 0:512], AL.add, AL.mult)
                    v.tensor_tensor_scan(kcsb[i % 2][:], ugv[:], ugv[:],
                                         kl_col[:, i:i + 1], AL.add,
                                         AL.bypass).then_inc(VE, 1)
                    v.wait_ge(VE, v0 + 2 * i + 1)
                    v.tensor_copy(kl_col[:, i:i + 1],
                                  kcsb[i % 2][:, 511:512]).then_inc(VE, 1)
                pipe(RC, t_emit, "vector", v_emit, e_incs=2, depth=4)
                @block.sync
                def _(sy, j=j, v0=v0):
                    for i in range(RC):
                        sy.wait_ge(VE, v0 + 2 * i + 2)
                        sy.dma_start(out=kc_sp[i, j],
                                     in_=kcsb[i % 2][:]).then_inc(SD, 16)
                bump("s", RC)
            vend[hidx] = c["v"]
        t_p2e = c["t"]

        vsz0 = c["v"]
        @block.vector
        def _(v):
            a3 = acols[:].rearrange("p (h i) -> p h i", i=16)
            v.tensor_add(srun[:], a3[:, 0, :], a3[:, 1, :]).then_inc(VE, 1)
            v.wait_ge(VE, vsz0 + 1)
            for h in range(2, 6):
                v.tensor_add(srun[:], srun[:], a3[:, h, :]).then_inc(VE, 1)
                v.wait_ge(VE, vsz0 + h)
            v.scalar_tensor_tensor(sza[:, 0:16], ccol[:], npz[:, 0:1], srun[:],
                                   AL.mult, AL.add).then_inc(VE, 1)
            v.tensor_copy(sza[:, 16:32], kl_col[:]).then_inc(VE, 1)
        v_sz = bump("v", 7)

        A.free("wu")
        A.alloc("wv", DC * 2048)
        A.alloc("Gh", RC * 512)

        @block.gpsimd
        def _(gp):
            gp.wait_ge(TMS, t_p2e)
            gp.dma_start(out=A.v3("wv", DC, 2048), in_=r3(wvT_d, 0, R)).then_inc(G_, 16)
        g_wv = bump("g") * 16

        @block.gpsimd
        def _(gp):
            gp.wait_ge(VE, v_sz)
            pid = gp.partition_id()
            with gp.register("role") as role:
                gp.reg_alu(role, pid, 1, AL.bitwise_and)
                with gp.If_eq(role, 0):
                    gp.dma_start(out=xch[0, 0], in_=sza[:, 0:16]).then_inc(G_, 16)
                    gp.dma_start(out=xch[0, 1], in_=sza[:, 16:32]).then_inc(G_, 16)
                with gp.Else():
                    gp.dma_start(out=xch[1, 0], in_=sza[:, 0:16]).then_inc(G_, 16)
                    gp.dma_start(out=xch[1, 1], in_=sza[:, 16:32]).then_inc(G_, 16)
                gp.wait_ge(G_, bump("g", 2) * 16)
                with gp.If_eq(role, 0):
                    gp.dma_start(out=xfl[0:1, :], in_=nonce_t[:]).then_inc(G_, 16)
                with gp.Else():
                    gp.dma_start(out=xfl[1:2, :], in_=nonce_t[:]).then_inc(G_, 16)
                gp.wait_ge(G_, bump("g") * 16)
                with gp.register("cntr") as cntr, gp.register("f0") as f0, \
                        gp.register("f1") as f1, gp.register("nd") as nd, \
                        gp.register("nd2") as nd2:
                    gp.reg_mov(cntr, 0)
                    gp.reg_mov(nd, 1)
                    with gp.While(nd):
                        with gp.If_eq(role, 0):
                            gp.dma_start(out=flag_t[:], in_=xfl[1:2, :]).then_inc(PXs, 16)
                        with gp.Else():
                            gp.dma_start(out=flag_t[:], in_=xfl[0:1, :]).then_inc(PXs, 16)
                        gp.reg_add(cntr, cntr, 16)
                        gp.wait_ge(PXs, cntr)
                        gp.load(f0, flag_t[0:1, 0:1])
                        gp.load(f1, flag_t[0:1, 1:2])
                        gp.reg_alu(nd, f0, nonce0, AL.not_equal)
                        gp.reg_alu(nd2, f1, nonce1, AL.not_equal)
                        gp.reg_alu(nd, nd, nd2, AL.bitwise_or)
                    with gp.If_eq(role, 0):
                        gp.dma_start(out=sza[:, 32:48], in_=xch[1, 0]).then_inc(G_, 16)
                        gp.dma_start(out=sza[:, 48:64], in_=xch[1, 1]).then_inc(G_, 16)
                    with gp.Else():
                        gp.dma_start(out=sza[:, 32:48], in_=xch[0, 0]).then_inc(G_, 16)
                        gp.dma_start(out=sza[:, 48:64], in_=xch[0, 1]).then_inc(G_, 16)
                    gp.wait_ge(G_, bump("g", 2) * 16)
                    gp.sem_inc(FB, 1)
        fb_x = bump("fb")

        voff0 = c["v"]
        @block.vector
        def _(v):
            v.wait_ge(FB, fb_x)
            v.tensor_add(offc[:], sza[:, 0:16], sza[:, 32:48]).then_inc(VE, 1)
            v.wait_ge(VE, voff0 + 1)
            v.scalar_tensor_tensor(offc[:], sza[:, 48:64], npz[:, 1:2], offc[:],
                                   AL.mult, AL.add).then_inc(VE, 1)
        bump("v", 2)

        if debug:
            vv = c["v"]
            @block.sync
            def _(sy):
                sy.wait_ge(VE, vv)
                sy.dma_start(out=dbg["off"][:], in_=offc[:]).then_inc(SD, 16)
                sy.dma_start(out=dbg["srun"][:], in_=srun[:]).then_inc(SD, 16)
                sy.dma_start(out=dbg["ccol"][:], in_=ccol[:]).then_inc(SD, 16)
            bump("s", 3)

        s_gsp = [0]
        for j in range(2):
            hidx = 8 + j
            pv = vend[hidx - 2]
            xrn, gtn = "xrc%d" % (hidx % 2), "gtc%d" % (hidx % 2)
            @block.sync
            def _(sy, j=j, pv=pv, xrn=xrn, gtn=gtn):
                sy.wait_ge(VE, pv)
                sy.dma_start(out=A.ap(xrn)[:, 0:DC * 512],
                             in_=xn_sp[6 + j]).then_inc(SD, 16)
                sy.dma_start(out=A.ap(gtn)[:, 0:DC * 512],
                             in_=gt_sp[6 + j]).then_inc(SD, 16)
            s_ld = bump("s", 2) * 16
            s_kc0 = c["s"]
            v0j = c["v"]
            @block.sync
            def _(sy, j=j, v0j=v0j, s_kc0=s_kc0):
                kcv = A.f32("kcc").rearrange("p (c t) -> p c t", t=512)
                for i in range(RC):
                    sy.wait_ge(VE, v0j + i - 1 if i >= 2 else v0j)
                    sy.dma_start(out=kcv[:, i % 2, 0:512],
                                 in_=kc_sp[i, j]).then_inc(SD, 16)
            bump("s", RC)

            def t_emit(t, i, s_ld=s_ld, j=j, xrn=xrn):
                if i == 0:
                    t.wait_ge(SD, s_ld)
                    if j == 0:
                        t.wait_ge(G_, g_wv)
                wv = A.v3("wv", DC, 2048)
                xr = A.v3(xrn, DC, 512)
                for d in range(DC):
                    mm = t.matmul(PM[i % 4][:, 0:512], wv[:, d, i * 128:(i + 1) * 128],
                                  xr[:, d, 0:512], start=(d == 0), stop=(d == DC - 1))
                mm.then_inc(TMS, 1)

            def v_emit(v, i, j=j, s_kc0=s_kc0, gtn=gtn):
                gt = A.v3(gtn, DC, 512)
                kcv = A.f32("kcc").rearrange("p (c t) -> p c t", t=512)
                Gv = A.v3("Gh", RC, 512)
                v.wait_ge(SD, (s_kc0 + i + 1) * 16)
                if j == 1 and i == 0:
                    v.wait_ge(SD, s_gsp[0])
                v.scalar_tensor_tensor(ugv[:], PM[i % 4][:, 0:512],
                                       bcols[:, B_BV + i:B_BV + i + 1],
                                       gt[:, i, 0:512], AL.add, AL.mult)
                v.scalar_tensor_tensor(Gv[:, i, 0:512],
                                       kcv[:, i % 2, 0:512], offc[:, i:i + 1],
                                       ugv[:], AL.add, AL.mult).then_inc(VE, 1)
            pipe(RC, t_emit, "vector", v_emit, depth=4)
            if j == 0:
                ve_h0 = c["v"]
                @block.sync
                def _(sy, ve_h0=ve_h0):
                    sy.wait_ge(VE, ve_h0)
                    sy.dma_start(out=g_sp[:], in_=A.ap("Gh")[:, 0:RC * 512]).then_inc(SD, 16)
                s_gsp[0] = bump("s") * 16

        barrier()
        A.free("wv", "gtc0", "gtc1", "kcc", "xrc0", "xrc1")

        A.alloc("wub", RC * 2048)
        A.alloc("cvc", DC * 512)
        A.alloc("attn", DC * 1024)

        s_wub = (c["s"] + 1) * 16
        @block.sync
        def _(sy):
            sy.dma_start(out=A.ap("wub")[:, 0:RC * 2048],
                         in_=wub_sp.rearrange("(r p) d -> p r d", p=128)).then_inc(SD, 16)
        bump("s")

        for j in (1, 0):
            v_prev = c["v"]
            t_prev = c["t"]
            @block.sync
            def _(sy, j=j, v_prev=v_prev, t_prev=t_prev):
                sy.wait_ge(VE, v_prev)
                sy.dma_start(out=A.ap("cvc")[:, 0:DC * 512],
                             in_=cv_sp[j]).then_inc(SD, 16)
                if j == 0:
                    sy.wait_ge(TMS, t_prev)
                    sy.dma_start(out=A.ap("Gh")[:, 0:RC * 512],
                                 in_=g_sp[:]).then_inc(SD, 16)
            s_cv = bump("s", 1 if j == 1 else 2) * 16
            for gix in range(4):
                v_evac_prev = c["v"]
                @block.tensor
                def _(t, gix=gix, j=j, s_cv=s_cv, v_evac_prev=v_evac_prev):
                    t.wait_ge(VE, v_evac_prev)
                    if gix == 0:
                        t.wait_ge(SD, s_cv)
                        t.wait_ge(SD, s_wub)
                    wubv = A.ap("wub")[:, 0:RC * 2048].rearrange(
                        "p (r d) -> p r d", d=2048)
                    Gv = A.v3("Gh", RC, 512)
                    for dd in range(4):
                        d = gix * 4 + dd
                        for rr in range(RC):
                            mm = t.matmul(PM[dd][:, 0:512],
                                          wubv[:, rr, d * 128:(d + 1) * 128],
                                          Gv[:, rr, 0:512],
                                          start=(rr == 0), stop=(rr == RC - 1))
                        mm.then_inc(TMS, 1)
                bump("t", 4)
                t_done = c["t"]
                @block.vector
                def _(v, gix=gix, j=j, t_done=t_done, s_cv=s_cv):
                    v.wait_ge(TMS, t_done)
                    v.wait_ge(SD, s_cv)
                    at = A.v3("attn", DC, 1024)
                    cv = A.v3("cvc", DC, 512)
                    for dd in range(4):
                        i = v.tensor_add(at[:, gix * 4 + dd, j * 512:(j + 1) * 512],
                                         PM[dd][:, 0:512], cv[:, gix * 4 + dd, 0:512])
                    i.then_inc(VE, 1)
                bump("v")

        barrier()
        if debug:
            sd_at = c["s"]
            @block.sync
            def _(sy):
                sy.dma_start(out=dbg["atn"][:],
                             in_=A.ap("attn")[:, 0:DC * 1024]).then_inc(SD, 16)
            bump("s")
            barrier()
        A.free("wub", "cvc", "Gh")

        A.alloc("sqb0", DC * 512)
        A.alloc("sqb1", DC * 512)
        A.alloc("xbm", DC * 1024)

        @block.gpsimd
        def _(gp):
            gp.dma_start(out=A.v3("xbm", DC, 1024),
                         in_=xm_d[:, :, 1:1025]).then_inc(G_, 16)
        g_xb = bump("g") * 16

        at3 = lambda: A.v3("attn", DC, 1024)
        for j in range(2):
            v_n = emit_ln(lambda d, j=j: at3()[:, d, j * 512:(j + 1) * 512],
                          512, 0, lambda d, j=j: at3()[:, d, j * 512:(j + 1) * 512])
            @block.vector
            def _(v, j=j, v_n=v_n):
                v.wait_ge(VE, v_n)
                v.wait_ge(G_, g_xb)
                at = at3()
                xb3 = A.v3("xbm", DC, 1024)
                for d in range(DC):
                    i = v.tensor_add(at[:, d, j * 512:(j + 1) * 512],
                                     at[:, d, j * 512:(j + 1) * 512],
                                     xb3[:, d, j * 512:(j + 1) * 512])
                i.then_inc(VE, 1)
            bump("v")

        v_x1 = c["v"]
        @block.sync
        def _(sy):
            sy.wait_ge(VE, v_x1)
            sy.dma_start(out=x1_sp[:], in_=A.ap("attn")[:, 0:DC * 1024]).then_inc(SD, 16)
        bump("s")

        for j in range(2):
            emit_ln(lambda d, j=j: at3()[:, d, j * 512:(j + 1) * 512], 512, 0,
                    lambda d, j=j: A.v3("xbm", DC, 1024)[:, d, j * 512:(j + 1) * 512])

        barrier()
        A.free("attn", "sqb0", "sqb1")
        A.rename("xbm", "xn2")

        A.alloc("gtw", DC * 8)

        @block.gpsimd
        def _(gp):
            gp.dma_start(out=A.v3("gtw", DC, 8),
                         in_=gateT_d.rearrange("(c p) e -> p c e", p=128)).then_inc(G_, 16)
            gp.dma_start(out=buRb[:].rearrange("e (c d) -> e c d", d=128),
                         in_=buR_d.rearrange("c e d -> e c d")).then_inc(G_, 16)
        g_gt = bump("g", 2) * 16

        xn23 = lambda: A.v3("xn2", DC, 1024)
        sc_lg0 = c["sc"]
        t_lg0 = c["t"]
        @block.tensor
        def _(t):
            t.wait_ge(G_, g_gt)
            gtw = A.v3("gtw", DC, 8)
            for j in range(2):
                if j == 1:
                    t.wait_ge(SE, sc_lg0 + 1)
                for d in range(DC):
                    i = t.matmul(PM[2][0:8, 0:512], gtw[:, d, 0:8],
                                 xn23()[:, d, j * 512:(j + 1) * 512],
                                 start=(d == 0), stop=(d == DC - 1))
                i.then_inc(TMS, 1)
        bump("t", 2)
        @block.scalar
        def _(s):
            for j in range(2):
                s.wait_ge(TMS, t_lg0 + j + 1)
                s.copy(lgr[:, j * 512:(j + 1) * 512], PM[2][0:8, 0:512]).then_inc(SE, 1)
        sc_lg = bump("sc", 2)

        def t_tr(t, q):
            if q == 0:
                t.wait_ge(SE, sc_lg)
            t.transpose(PM[q % 2][:, 0:8], lgr[:, q * 128:(q + 1) * 128],
                        idf[0:8, 0:8]).then_inc(TMS, 1)

        def s_tr(s, q):
            s.copy(lgm[:, q, :], PM[q % 2][:, 0:8]).then_inc(SE, 1)
        pipe(8, t_tr, "scalar", s_tr)
        sc_tra = c["sc"]

        vg0 = c["v"]
        @block.vector
        def _(v):
            v.wait_ge(SE, sc_tra)
            for q in range(8):
                i = v.reduce_max(m1a[:, q:q + 1], lgm[:, q, :], AXX)
            i.then_inc(VE, 1)
            v.wait_ge(VE, vg0 + 1)
            for q in range(8):
                v.tensor_scalar(cmt[:, q, :], lgm[:, q, :], m1a[:, q:q + 1],
                                None, AL.subtract)
            for q in range(8):
                v.tensor_single_scalar(qxp[:, q, :], cmt[:, q, :], 0.0, AL.is_ge)
            for q in range(8):
                v.scalar_tensor_tensor(qxp[:, q, :], qxp[:, q, :], -1e30,
                                       cmt[:, q, :], AL.mult, AL.add)
            for q in range(8):
                i = v.reduce_max(m2a[:, q:q + 1], qxp[:, q, :], AXX)
            i.then_inc(VE, 1)
        v_cm1 = bump("v", 2)

        @block.scalar
        def _(s):
            s.wait_ge(VE, v_cm1)
            for q in range(8):
                s.activation(qxp[:, q, :], cmt[:, q, :], AF.Exp)
            s.activation(e2a[:], m2a[:], AF.Exp).then_inc(SE, 1)
        sc_cm = bump("sc")

        @block.vector
        def _(v):
            v.wait_ge(SE, sc_cm)
            v.tensor_single_scalar(gsc[:], e2a[:], 1.0, AL.add)
            for q in range(8):
                v.tensor_scalar(lgm[:, q, :], cmt[:, q, :], m2a[:, q:q + 1],
                                None, AL.is_ge)
            v.reciprocal(e2a[:], gsc[:])
            for q in range(8):
                v.tensor_tensor(qxp[:, q, :], qxp[:, q, :], lgm[:, q, :], AL.mult)
            for q in range(8):
                i = v.tensor_scalar(cmt[:, q, :], qxp[:, q, :], e2a[:, q:q + 1],
                                    None, AL.mult)
            i.then_inc(VE, 1)
        v_cm2 = bump("v")

        def t_trb(t, q):
            if q == 0:
                t.wait_ge(VE, v_cm2)
            t.transpose(PM[q % 2][0:8, 0:128], cmt[:, q, :], idf[0:128, 0:128]).then_inc(TMS, 1)

        def s_trb(s, q):
            s.copy(cmr[:, q * 128:(q + 1) * 128], PM[q % 2][0:8, 0:128]).then_inc(SE, 1)
        pipe(8, t_trb, "scalar", s_trb)
        sc_trb = c["sc"]

        @block.vector
        def _(v):
            v.wait_ge(SE, sc_trb)
            v.tensor_copy(cmb[:], cmr[:]).then_inc(VE, 1)
        v_cmb = bump("v")

        if debug:
            @block.sync
            def _(sy):
                sy.wait_ge(VE, v_cmb)
                sy.dma_start(out=dbg["comb"][:], in_=cmr[:]).then_inc(SD, 16)
            bump("s")

        barrier()

        A.free("gtw")
        A.alloc("hTa", E * BC * 512)
        A.alloc("hTb", E * BC * 512)
        A.alloc("sh", BC * 1024)
        A.alloc("wda", DC * 512)
        A.alloc("wdb", DC * 512)
        A.alloc("Bb", 2 * 1024)

        def hT_ap(k):
            if k < 16:
                return A.v3("hTa", 16, 1024)[:, k, :]
            return A.v3("hTb", 16, 1024)[:, k - 16, :]

        g_wd = []
        t_p8 = c["t"]
        @block.gpsimd
        def _(gp):
            for e_ in range(E + 1):
                if e_ >= 2:
                    gp.wait_ge(TMS, t_p8 + 9 * (e_ - 1) - (1 if e_ == E + 0 and False else 0))
                gp.dma_start(out=A.v3("wda" if e_ % 2 == 0 else "wdb", DC, 512),
                             in_=wdT_d[e_].rearrange("(c p) b -> p c b", p=128)).then_inc(G_, 16)
                g_wd.append(bump("g") * 16)

        vend = [0, 0]
        for e_ in range(E + 1):
            slab = "wda" if e_ % 2 == 0 else "wdb"
            is_sh = (e_ == E)
            t0 = c["t"]
            v_base = c["v"]
            sc_base = c["sc"]
            @block.tensor
            def _(t, e_=e_, slab=slab, sc_base=sc_base, is_sh=is_sh):
                t.wait_ge(G_, g_wd[e_])
                t.wait_ge(SE, sc_base)
                if not is_sh:
                    t.matmul(PBb[0][:, 0:512], oneh[0:8, e_ * 128:(e_ + 1) * 128],
                             cmb[0:8, 0:512], start=True, stop=True)
                    t.matmul(PBb[1][:, 0:512], oneh[0:8, e_ * 128:(e_ + 1) * 128],
                             cmb[0:8, 512:1024], start=True,
                             stop=True).then_inc(TMS, 1)
                wd = A.v3(slab, DC, 512)
                for i in range(8):
                    j, bc = i // 4, i % 4
                    if i >= 4:
                        t.wait_ge(SE, sc_base + i - 2)
                    for d in range(DC):
                        mm = t.matmul(PM[bc][:, 0:512],
                                      wd[:, d, bc * 128:(bc + 1) * 128],
                                      xn23()[:, d, j * 512:(j + 1) * 512],
                                      start=(d == 0), stop=(d == DC - 1))
                    mm.then_inc(TMS, 1)
            bump("t", 8 if is_sh else 9)
            @block.scalar
            def _(s, e_=e_, t0=t0, is_sh=is_sh, v_base=v_base):
                s.wait_ge(VE, v_base)
                if not is_sh:
                    s.wait_ge(TMS, t0 + 1)
                    s.wait_ge(VE, vend[0])
                    Bb3 = A.v3("Bb", 2, 1024)
                    s.copy(Bb3[:, e_ % 2, 0:512], PBb[0][:, 0:512])
                    s.copy(Bb3[:, e_ % 2, 512:1024], PBb[1][:, 0:512]).then_inc(SE, 1)
                tb = 1 if not is_sh else 0
                for i in range(8):
                    j, bc = i // 4, i % 4
                    s.wait_ge(TMS, t0 + tb + i + 1)
                    if is_sh:
                        s.activation(A.v3("sh", BC, 1024)[:, bc, j * 512:(j + 1) * 512],
                                     PM[bc][:, 0:512], AF.Gelu,
                                     bias=bcols[:, B_BSD + bc:B_BSD + bc + 1]).then_inc(SE, 1)
                    else:
                        if i >= 2:
                            s.wait_ge(VE, v_base + i - 1)
                        s.activation(gelb[i % 2][:], PM[bc][:, 0:512], AF.Gelu,
                                     bias=bcols[:, B_BD + e_ * 4 + bc:
                                                B_BD + e_ * 4 + bc + 1]).then_inc(SE, 1)
            bump("sc", 8 if is_sh else 9)
            if not is_sh:
                sc_g0 = c["sc"] - 8
                @block.vector
                def _(v, e_=e_, sc_g0=sc_g0):
                    for i in range(8):
                        j, bc = i // 4, i % 4
                        v.wait_ge(SE, sc_g0 + i + 1)
                        v.tensor_mul(hT_ap(e_ * BC + bc)[:, j * 512:(j + 1) * 512],
                                     gelb[i % 2][:],
                                     A.v3("Bb", 2, 1024)[:, e_ % 2,
                                                         j * 512:(j + 1) * 512]
                                     ).then_inc(VE, 1)
                bump("v", 8)
            vend = [vend[1], c["v"]]

        barrier()
        A.free("wda", "wdb", "Bb", "xn2")

        A.alloc("wt0", 36 * 128)
        A.alloc("wt1", 36 * 128)
        A.alloc("ffna", 8 * 1024)
        A.alloc("ffnb", 8 * 1024)
        A.alloc("sq9", 1024)

        def ffn_ap(dc):
            if dc < 8:
                return A.v3("ffna", 8, 1024)[:, dc, :]
            return A.v3("ffnb", 8, 1024)[:, dc - 8, :]

        g_wt = []
        t_p9 = c["t"]
        @block.gpsimd
        def _(gp):
            for dc in range(DC):
                if dc >= 2:
                    gp.wait_ge(TMS, t_p9 + 3 * (dc - 2) + 2)
                w = "wt%d" % (dc % 2)
                gp.dma_start(out=A.ap(w)[:, 0:36 * 128],
                             in_=wuPM_d[dc]).then_inc(G_, 16)
                g_wt.append(bump("g") * 16)

        sc_p9 = c["sc"]
        for dc in range(DC):
            w = "wt%d" % (dc % 2)
            t0 = c["t"]
            @block.tensor
            def _(t, dc=dc, w=w):
                t.wait_ge(G_, g_wt[dc])
                if dc == 0:
                    t.wait_ge(G_, g_gt)
                wt = A.v3(w, 36, 128)
                sh = A.v3("sh", BC, 1024)
                for j in range(2):
                    if dc > 0:
                        t.wait_ge(SE, sc_p9 + 2 * (dc - 1) + j + 1)
                    for k in range(32):
                        t.matmul(PM[j][:, 0:512], wt[:, k, :],
                                 hT_ap(k)[:, j * 512:(j + 1) * 512],
                                 start=(k == 0), stop=False)
                    for bcx in range(4):
                        t.matmul(PM[j][:, 0:512], wt[:, 32 + bcx, :],
                                 sh[:, bcx, j * 512:(j + 1) * 512],
                                 start=False, stop=False)
                    t.matmul(PM[j][:, 0:512], buRb[:, dc * 128:(dc + 1) * 128],
                             cmb[:, j * 512:(j + 1) * 512],
                             start=False, stop=True).then_inc(TMS, 1)
            bump("t", 2)
            @block.scalar
            def _(s, dc=dc, t0=t0):
                for j in range(2):
                    s.wait_ge(TMS, t0 + j + 1)
                    s.activation(ffn_ap(dc)[:, j * 512:(j + 1) * 512],
                                 PM[j][:, 0:512], AF.Identity,
                                 bias=bcols[:, B_BSU + dc:B_BSU + dc + 1])
                    s.activation(A.v3("sq9", 1, 1024)[:, 0, j * 512:(j + 1) * 512],
                                 PM[j][:, 0:512], AF.Square,
                                 bias=bcols[:, B_BSU + dc:B_BSU + dc + 1]).then_inc(SE, 1)
            bump("sc", 2)
            sc_e = c["sc"]
            @block.tensor
            def _(t, dc=dc, sc_e=sc_e):
                for j in range(2):
                    t.wait_ge(SE, sc_e - 1 + j)
                    st_sum = PR[0][0:1, 0:512] if j == 0 else PBb[0][0:1, 0:512]
                    st_sq = PR[1][0:1, 0:512] if j == 0 else PBb[1][0:1, 0:512]
                    t.matmul(st_sum, ones_col[:],
                             ffn_ap(dc)[:, j * 512:(j + 1) * 512],
                             start=(dc == 0), stop=(dc == DC - 1), skip_group_check=True)
                    i = t.matmul(st_sq, ones_col[:],
                                 A.v3("sq9", 1, 1024)[:, 0, j * 512:(j + 1) * 512],
                                 start=(dc == 0), stop=(dc == DC - 1),
                                 skip_group_check=True)
                i.then_inc(TMS, 1)
            bump("t", 1)
        t_stats = c["t"]

        A.free("hTa", "hTb", "sh", "wt0", "wt1")
        A.alloc("x1c", DC * 1024)
        @block.sync
        def _(sy):
            sy.wait_ge(TMS, t_stats)
            sy.dma_start(out=A.ap("x1c")[:, 0:DC * 1024], in_=x1_sp[:]).then_inc(SD, 16)
        s_x1 = bump("s") * 16

        for j, abx in ((0, ab_bf), (1, ab2_bf)):
            su = PR[0][0:1, 0:512] if j == 0 else PBb[0][0:1, 0:512]
            sq = PR[1][0:1, 0:512] if j == 0 else PBb[1][0:1, 0:512]
            @block.vector
            def _(v, su=su, sq=sq):
                v.wait_ge(TMS, t_stats)
                v.wait_ge(SE, c["sc"])
                m, ms = strow[0:1, 0:512], strow[0:1, 512:1024]
                var = strow[0:1, 1024:1536]
                v.tensor_single_scalar(m, su, 1.0 / D, AL.mult)
                v.tensor_single_scalar(ms, sq, 1.0 / D, AL.mult)
                v.tensor_tensor(var, m, m, AL.mult)
                v.tensor_sub(var, ms, var).then_inc(VE, 1)
            vf1 = bump("v")
            @block.scalar
            def _(s, vf1=vf1):
                s.wait_ge(VE, vf1)
                s.activation(strow[0:1, 1536:2048], strow[0:1, 1024:1536],
                             AF.Sqrt, bias=eps_col[0:1, 0:1]).then_inc(SE, 1)
            sf1 = bump("sc")
            @block.vector
            def _(v, abx=abx, sf1=sf1):
                v.wait_ge(SE, sf1)
                a, b = strow[0:1, 2048:2560], strow[0:1, 2560:3072]
                v.reciprocal(a, strow[0:1, 1536:2048])
                v.scalar_tensor_tensor(b, strow[0:1, 0:512], -1.0, a, AL.mult, AL.mult)
                v.tensor_copy(abx[0:1, 0:512], a)
                v.tensor_copy(abx[0:1, 512:1024], b).then_inc(VE, 1)
            vf2 = bump("v")
            @block.tensor
            def _(t, abx=abx, vf2=vf2):
                t.wait_ge(VE, vf2)
                t.matmul(PM[2][:, 0:512], ones_row[:], abx[0:1, 0:512],
                         start=True, stop=True)
                t.matmul(PM[3][:, 0:512], ones_row[:], abx[0:1, 512:1024],
                         start=True, stop=True).then_inc(TMS, 1)
            tb = bump("t")
            @block.scalar
            def _(s, tb=tb):
                s.wait_ge(TMS, tb)
                s.copy(ABt[:, 0:512], PM[2][:, 0:512])
                s.copy(ABt[:, 512:1024], PM[3][:, 0:512]).then_inc(SE, 1)
            sb = bump("sc")
            v0 = c["v"]
            s0y = c["s"]
            @block.vector
            def _(v, j=j, sb=sb, s0y=s0y):
                v.wait_ge(SE, sb)
                v.wait_ge(SD, s_x1)
                x13 = A.v3("x1c", DC, 1024)
                for dc in range(DC):
                    if dc >= 2:
                        v.wait_ge(SD, (s0y + dc - 1) * 16)
                    elif j == 1:
                        v.wait_ge(SD, s0y * 16)
                    v.tensor_mul(ugv[:], ffn_ap(dc)[:, j * 512:(j + 1) * 512],
                                 ABt[:, 0:512])
                    v.tensor_add(ugv[:], ugv[:], ABt[:, 512:1024])
                    v.tensor_add(outb[dc % 2][:], ugv[:],
                                 x13[:, dc, j * 512:(j + 1) * 512]).then_inc(VE, 1)
            bump("v", DC)
            @block.sync
            def _(sy, j=j, v0=v0):
                for dc in range(DC):
                    sy.wait_ge(VE, v0 + dc + 1)
                    sy.dma_start(out=y_d[dc * 128:(dc + 1) * 128,
                                         j * 512:(j + 1) * 512],
                                 in_=outb[dc % 2][:]).then_inc(SD, 16)
            bump("s", DC)

        if debug:
            @block.sync
            def _(sy):
                for nm, src in (("wub", wub_sp), ("gsp", g_sp), ("xn", xn_sp), ("gt", gt_sp),
                                ("kc", kc_sp), ("cv", cv_sp), ("x1", x1_sp)):
                    sy.dma_start(out=dbg[nm][:], in_=src[:]).then_inc(SD, 16)
            bump("s", 5)

        barrier()

    return nc, dbg


def _build_wupm(wu, ws_u):
    f32 = np.float32
    wuR = wu.reshape(E, DC, 128, BC, 128).transpose(1, 0, 3, 4, 2)
    a = wuR.transpose(0, 3, 1, 2, 4).reshape(DC, 128, 32 * 128)
    wsuR = ws_u.T.reshape(BC, 128, DC, 128).transpose(2, 0, 1, 3)
    b = wsuR.transpose(0, 2, 1, 3).reshape(DC, 128, 4 * 128)
    return np.ascontiguousarray(np.concatenate([a, b], axis=2), dtype=f32)


def make_inputs(inputs):
    f32 = np.float32
    x = np.ascontiguousarray(np.asarray(inputs["L0_attn_x"], dtype=f32)[0].T)
    Lg = x.shape[1]
    Wu = np.asarray(inputs["L0_attn_Wu"], f32)
    Wv = np.asarray(inputs["L0_attn_Wv"], f32)
    Wg1 = np.asarray(inputs["L0_attn_Wg1"], f32)
    Wg2 = np.asarray(inputs["L0_attn_Wg2"], f32)
    conv_w = np.asarray(inputs["L0_attn_conv_w"], f32)
    gate_w = np.asarray(inputs["L0_ffn_gate_w"], f32)
    wd = np.asarray(inputs["L0_ffn_wd"], f32)
    wu = np.asarray(inputs["L0_ffn_wu"], f32)
    ws_d = np.asarray(inputs["L0_ffn_ws_d"], f32)
    ws_u = np.asarray(inputs["L0_ffn_ws_u"], f32)

    shared = {
        "wuT": np.ascontiguousarray(Wu.T),
        "wvT": np.ascontiguousarray(Wv.T),
        "wg1T": np.ascontiguousarray(Wg1.T),
        "wg2T": np.ascontiguousarray(Wg2.T),
        "gateT": np.ascontiguousarray(gate_w.T),
        "wdT": np.ascontiguousarray(np.concatenate(
            [np.transpose(wd, (0, 2, 1)), ws_d.T[None]], axis=0)),
        "wuPM": _build_wupm(wu, ws_u),
        "buR": np.ascontiguousarray(
            np.asarray(inputs["L0_ffn_bu"], f32).reshape(E, DC, 128).transpose(1, 0, 2)),
    }
    brows = np.zeros((NBR, 128), f32)
    brows[B_BU:B_BU + 16] = np.asarray(inputs["L0_attn_bu"], f32).reshape(16, 128)
    brows[B_BV:B_BV + 16] = np.asarray(inputs["L0_attn_bv"], f32).reshape(16, 128)
    brows[B_BG2:B_BG2 + 16] = np.asarray(inputs["L0_attn_bg2"], f32).reshape(16, 128)
    brows[B_BG1:B_BG1 + 4] = np.asarray(inputs["L0_attn_bg1"], f32).reshape(4, 128)
    brows[B_CB:B_CB + 16] = np.asarray(inputs["L0_attn_conv_b"], f32).reshape(16, 128)
    brows[B_BSD:B_BSD + 4] = np.asarray(inputs["L0_ffn_bs_d"], f32).reshape(4, 128)
    brows[B_BSU:B_BSU + 16] = np.asarray(inputs["L0_ffn_bs_u"], f32).reshape(16, 128)
    brows[B_BD:B_BD + 32] = np.asarray(inputs["L0_ffn_bd"], f32).reshape(32, 128)
    for k in range(3):
        brows[B_CW + 16 * k:B_CW + 16 * (k + 1)] = conv_w[:, 0, k].reshape(16, 128)
    shared["brows"] = brows
    oh = np.zeros((8, E * 128), f32)
    for e in range(E):
        oh[e, e * 128:(e + 1) * 128] = 1.0
    shared["onehots"] = oh

    in_maps = []
    for core in range(8):
        p, q = core // 2, core & 1
        s = 2048 * p + 1024 * q
        xm = np.zeros((D, TM + 2), f32)
        lo, hi = s - 1, s + TM + 1
        a, b = max(lo, 0), min(hi, Lg)
        xm[:, a - lo:a - lo + (b - a)] = x[:, a:b]
        pl = 1024 * p
        xp = np.zeros((D, TP), f32)
        if pl:
            xp[:, TP - pl:] = x[:, q * pl:(q + 1) * pl]
        npc = np.zeros((128, 2), f32)
        npc[:, 0] = -(TP - pl)
        npc[:, 1] = float(q)
        m = dict(shared)
        m["xm"] = np.ascontiguousarray(
            xm.reshape(DC, 128, TM + 2).transpose(1, 0, 2))
        m["xp"] = np.ascontiguousarray(
            xp.reshape(DC, 128, 6, 512).transpose(2, 1, 0, 3).reshape(6, 128, DC * 512))
        m["npcols"] = npc
        in_maps.append(m)
    return in_maps


def kernel(**inputs):
    rng = np.random.default_rng()
    nonce0 = int(rng.integers(1, 2**31 - 1))
    nonce1 = int(rng.integers(1, 2**31 - 1))
    in_maps = make_inputs(inputs)
    nc, _ = build_program(nonce0, nonce1, debug=False)
    res = run_bass_kernel_spmd(nc, in_maps, core_ids=list(range(8)))
    out = np.empty((1, 8192, D), np.float32)
    for core in range(8):
        p, q = core // 2, core & 1
        s = 2048 * p + 1024 * q
        out[0, s:s + TM, :] = np.asarray(res.results[core]["y"]).T
    return out


# version 07
# speedup vs baseline: 1.7466x; 1.7466x over previous
import contextlib

import numpy as np

import concourse.bass as bass
import concourse.mybir as mybir
from concourse.bass_utils import run_bass_kernel_spmd

F32 = mybir.dt.float32
BF16 = mybir.dt.bfloat16
I32 = mybir.dt.int32
AL = mybir.AluOpType
AF = mybir.ActivationFunctionType
AXX = mybir.AxisListType.X

D, DC = 2048, 16
R, RC = 2048, 16
DH, HC = 512, 4
E, BOT, BC = 8, 512, 4
TM, TP = 1024, 3072
EPS = 1e-5

B_BU, B_BV, B_BG2, B_BG1, B_CB, B_BSD, B_BSU, B_BD, B_CW = (
    0, 16, 32, 48, 52, 68, 72, 88, 120)
NBR = 168
ARENA_COLS = 77400


class Arena:
    def __init__(self, t):
        self.t = t
        self.live = {}

    def alloc(self, name, cols):
        cols = (cols + 511) // 512 * 512
        off = 0
        for s, e_ in sorted(self.live.values()):
            if off + cols <= s:
                break
            off = max(off, e_)
        assert off + cols <= ARENA_COLS, (
            f"arena overflow: {name} {cols} at {off}; live={self.live}")
        self.live[name] = (off, off + cols)

    def free(self, *names):
        for n in names:
            del self.live[n]

    def rename(self, old, new):
        self.live[new] = self.live.pop(old)

    def ap(self, name):
        s, e_ = self.live[name]
        return self.t[:, s:e_]

    def v3(self, name, nch, w):
        s, e_ = self.live[name]
        assert nch * w <= e_ - s, (name, nch, w, e_ - s)
        return self.t[:, s:s + nch * w].rearrange("p (c t) -> p c t", t=w)

    def f32(self, name):
        s, e_ = self.live[name]
        return self.t[:, s:e_].bitcast(F32)


def build_program(nonce0, nonce1, debug=False):
    nc = bass.Bass("TRN2", target_bir_lowering=False, debug=False,
                   detect_race_conditions=False)

    din = lambda n, s, d=F32: nc.dram_tensor(n, s, d, kind="ExternalInput").ap()
    xm_d = din("xm", [128, DC, TM + 2])
    xp_d = din("xp", [6, 128, DC * 512])
    wuT_d = din("wuT", [D, R])
    wvT_d = din("wvT", [D, R])
    wg1T_d = din("wg1T", [D, DH])
    wg2T_d = din("wg2T", [DH, R])
    gateT_d = din("gateT", [D, E])
    wdT_d = din("wdT", [E + 1, D, BOT])
    wuPM_d = din("wuPM", [DC, 128, 36 * 128])
    buR_d = din("buR", [DC, E, 128])
    brows_d = din("brows", [NBR, 128])
    npcols_d = din("npcols", [128, 2])
    oneh_d = din("onehots", [8, E * 128])
    y_d = nc.dram_tensor("y", [D, TM], F32, kind="ExternalOutput").ap()

    dsc = lambda n, s, d=BF16: nc.dram_tensor(n, s, d, kind="Internal",
                                              addr_space="Local").ap()
    xn_sp = dsc("xn_sp", [8, 128, DC * 512])
    gt_sp = dsc("gt_sp", [8, 128, DC * 512])
    cv_sp = dsc("cv_sp", [2, 128, DC * 512])
    kc_sp = dsc("kc_sp", [RC, 2, 128, 512], F32)
    x1_sp = dsc("x1_sp", [128, DC * 1024])
    g_sp = dsc("g_sp", [128, RC * 512])
    wub_sp = dsc("wub_sp", [D, R])
    xch = nc.dram_tensor("xch", [2, 2, 128, 16], F32, kind="Internal",
                         addr_space="Shared").ap()
    xfl = nc.dram_tensor("xfl", [2, 2], I32, kind="Internal",
                         addr_space="Shared").ap()

    dbg = {}
    if debug:
        for nm, shp in (("off", [128, 16]), ("srun", [128, 16]),
                        ("ccol", [128, 16]), ("comb", [E, TM])):
            dbg[nm] = nc.dram_tensor("dbg_" + nm, shp, F32,
                                     kind="ExternalOutput").ap()
        for nm, src_shape, dt_ in (("wub", [D, R], BF16),
                                   ("gsp", [128, RC * 512], BF16),
                                   ("atn", [128, DC * 1024], BF16),
                                   ("xn", [8, 128, DC * 512], BF16),
                                   ("gt", [8, 128, DC * 512], BF16),
                                   ("kc", [RC, 2, 128, 512], F32),
                                   ("cv", [2, 128, DC * 512], BF16),
                                   ("x1", [128, DC * 1024], BF16)):
            dbg[nm] = nc.dram_tensor("dbg_" + nm, src_shape, dt_,
                                     kind="ExternalOutput").ap()

    es = contextlib.ExitStack()
    SB = lambda n, s, d=BF16: es.enter_context(nc.sbuf_tensor(n, s, d))
    PSA = lambda n, s, d=F32: es.enter_context(nc.psum_tensor(n, s, d))
    SEM = lambda n: es.enter_context(nc.semaphore(n))

    idf = SB("idf", [128, 128], F32)
    iota_t = SB("iota_t", [128, 128], I32)
    ones_col = SB("ones_col", [128, 1], BF16)
    ones_row = SB("ones_row", [1, 128], BF16)
    brows1 = SB("brows1", [128, 128], F32)
    brows2 = SB("brows2", [NBR - 128, 128], F32)
    bcols = SB("bcols", [128, NBR], F32)
    ccol = SB("ccol", [128, 16], F32)
    offc = SB("offc", [128, 16], F32)
    sza = SB("sza", [128, 64], F32)
    srun = SB("srun", [128, 16], F32)
    kl_col = SB("kl_col", [128, 16], F32)
    acol = SB("acol", [128, 1], F32)
    acols = SB("acols", [128, 96], F32)
    npz = SB("npz", [128, 2], F32)
    eps_col = SB("eps_col", [1, 1], F32)
    nonce_t = SB("nonce_t", [1, 2], I32)
    flag_t = SB("flag_t", [1, 2], I32)
    strow = SB("strow", [1, 6 * 512], F32)
    ab_bf = SB("ab_bf", [1, 2 * 512], BF16)
    ab2_bf = SB("ab2_bf", [1, 2 * 512], BF16)
    ABt = SB("ABt", [128, 2 * 512], BF16)
    hb0 = SB("hb0", [128, 4], BF16)
    cgb = SB("cgb", [128, 16], BF16)
    hh = SB("hh", [128, HC, 512], BF16)
    ugv = SB("ugv", [128, 512], F32)
    kcsb = [SB("kcs%d" % i, [128, 512], F32) for i in range(2)]
    gelb = [SB("gel%d" % i, [128, 512], BF16) for i in range(2)]
    lgr = SB("lgr", [8, 1024], F32)
    cmr = SB("cmr", [8, 1024], F32)
    cmb = SB("cmb", [8, 1024], BF16)
    lgm = SB("lgm", [128, 8, 8], F32)
    cmt = SB("cmt", [128, 8, 8], F32)
    qxp = SB("qxp", [128, 8, 8], F32)
    m1a = SB("m1a", [128, 8], F32)
    m2a = SB("m2a", [128, 8], F32)
    e2a = SB("e2a", [128, 8], F32)
    gsc = SB("gsc", [128, 8], F32)
    buRb = SB("buRb", [8, DC * 128], BF16)
    oneh = SB("oneh", [8, E * 128], BF16)
    outb = [SB("outb%d" % i, [128, 512], F32) for i in range(2)]

    arena_t = SB("arena", [128, ARENA_COLS], BF16)
    A = Arena(arena_t)

    PM = [PSA("PM%d" % i, [128, 512]) for i in range(4)]
    PBb = [PSA("PBb%d" % i, [128, 512]) for i in range(2)]
    PR = [PSA("PR%d" % i, [1, 512]) for i in range(2)]

    PH = SEM("PH"); G_ = SEM("G_"); SD = SEM("SD"); TMS = SEM("TMS")
    VE = SEM("VE"); SE = SEM("SE"); FB = SEM("FB"); PXs = SEM("PXs")

    c = {"g": 0, "s": 0, "t": 0, "v": 0, "sc": 0, "fb": 0, "ph": 0}

    def bump(k, n=1):
        c[k] += n
        return c[k]

    with es, nc.Block() as block:

        def barrier():
            c["ph"] += 5
            tgt = c["ph"]
            sv, gv = c["s"] * 16, c["g"] * 16
            @block.sync
            def _(sy):
                sy.wait_ge(SD, sv)
                sy.sem_inc(PH, 1)
                sy.wait_ge(PH, tgt)
            @block.gpsimd
            def _(gp):
                gp.wait_ge(G_, gv)
                gp.sem_inc(PH, 1)
                gp.wait_ge(PH, tgt)
            tv, vv, sv2 = c["t"], c["v"], c["sc"]
            @block.tensor
            def _(t):
                t.wait_ge(TMS, tv)
                t.sem_inc(PH, 1)
                t.wait_ge(PH, tgt)
            @block.vector
            def _(v):
                v.wait_ge(VE, vv)
                v.sem_inc(PH, 1)
                v.wait_ge(PH, tgt)
            @block.scalar
            def _(s):
                s.wait_ge(SE, sv2)
                s.sem_inc(PH, 1)
                s.wait_ge(PH, tgt)

        def r3(dram2, a, b):
            return dram2[:, a:b].rearrange("(c p) t -> p c t", p=128)

        def pipe(n, t_emit, e_eng, e_emit, e_incs=1, depth=2):
            key = {"vector": "v", "scalar": "sc"}[e_eng]
            sem = {"vector": VE, "scalar": SE}[e_eng]
            t0, e0 = c["t"], c[key]
            @block.tensor
            def _(t):
                for i in range(n):
                    if i >= depth:
                        t.wait_ge(sem, e0 + (i - depth + 1) * e_incs)
                    t_emit(t, i)
            bump("t", n)
            sec = block.vector if e_eng == "vector" else block.scalar
            @sec
            def _(e):
                for i in range(n):
                    e.wait_ge(TMS, t0 + i + 1)
                    e_emit(e, i)
            bump(key, n * e_incs)
            return t0, e0

        A.alloc("xin0", DC * 512)
        A.alloc("xin1", DC * 512)
        A.alloc("sqb0", DC * 512)
        A.alloc("sqb1", DC * 512)
        A.alloc("xmr", DC * 1026)
        A.alloc("pxn", DC * 512)
        A.alloc("gth0", 512)
        A.alloc("gth1", 512)
        A.alloc("wg1", DC * 512)
        A.alloc("wg2", HC * 2048)

        @block.gpsimd
        def _(gp):
            gp.iota(iota_t[:], [[1, 128]], channel_multiplier=-1).then_inc(FB, 1)
            gp.dma_start(out=A.v3("xmr", DC, 1026),
                         in_=xm_d[:, :, 0:1026]).then_inc(G_, 16)
            gp.dma_start(out=oneh[:], in_=oneh_d[:]).then_inc(G_, 16)
            gp.dma_start(out=A.v3("wg1", DC, 512),
                         in_=r3(wg1T_d, 0, DH)).then_inc(G_, 16)
            gp.dma_start(out=A.v3("wg2", HC, 2048),
                         in_=wg2T_d.rearrange("(c p) r -> p c r", p=128)).then_inc(G_, 16)
        bump("fb")
        bump("g")
        g_xmr = c["g"] * 16
        g_wg = bump("g", 3) * 16

        @block.sync
        def _(sy):
            sy.dma_start(out=brows1[:], in_=brows_d[0:128, :]).then_inc(SD, 16)
            sy.dma_start(out=brows2[:], in_=brows_d[128:NBR, :]).then_inc(SD, 16)
            sy.dma_start(out=npz[:], in_=npcols_d[:]).then_inc(SD, 16)
        s_brows = bump("s", 3) * 16

        @block.vector
        def _(v):
            v.memset(ones_col[:], 1.0)
            v.memset(ones_row[:], 1.0)
            v.memset(srun[:], 0.0)
            v.memset(kl_col[:], 0.0)
            v.memset(nonce_t[0:1, 0:1], nonce0)
            v.memset(nonce_t[0:1, 1:2], nonce1)
            v.memset(eps_col[:], float(EPS))
            v.wait_ge(FB, 1)
            v.tensor_single_scalar(idf[:], iota_t[:], 0, AL.is_equal).then_inc(VE, 1)
        v_idf = bump("v")

        @block.tensor
        def _(t):
            t.wait_ge(VE, v_idf)
            t.wait_ge(SD, s_brows)
            t.transpose(PM[0][:, 0:128], brows1[:], idf[0:128, 0:128])
            t.transpose(PM[1][:, 0:NBR - 128], brows2[:],
                        idf[0:NBR - 128, 0:NBR - 128]).then_inc(TMS, 1)
        t_bias = bump("t")

        @block.scalar
        def _(s):
            s.wait_ge(TMS, t_bias)
            s.copy(bcols[:, 0:128], PM[0][:, 0:128])
            s.copy(bcols[:, 128:NBR], PM[1][:, 0:NBR - 128])
            s.activation(hb0[:], bcols[:, B_BG1:B_BG1 + 4], AF.Relu).then_inc(SE, 1)
        sc_hb0 = bump("sc")

        @block.tensor
        def _(t):
            t.wait_ge(SE, sc_hb0)
            t.wait_ge(G_, g_wg)
            wg2 = A.v3("wg2", HC, 2048)
            for r in range(RC):
                for h in range(HC):
                    i = t.matmul(PM[2][:, r:r + 1], wg2[:, h, r * 128:(r + 1) * 128],
                                 hb0[:, h:h + 1], start=(h == 0), stop=(h == HC - 1))
            i.then_inc(TMS, 1)
        t_cg = bump("t")

        @block.scalar
        def _(s):
            s.wait_ge(TMS, t_cg)
            for r in range(RC):
                i = s.activation(cgb[:, r:r + 1], PM[2][:, r:r + 1], AF.Sigmoid,
                                 bias=bcols[:, B_BG2 + r:B_BG2 + r + 1])
            i.then_inc(SE, 1)
        sc_cg = bump("sc")

        @block.vector
        def _(v):
            v.wait_ge(SE, sc_cg)
            v.tensor_mul(ccol[:], cgb[:], bcols[:, B_BU:B_BU + 16]).then_inc(VE, 1)
        bump("v")

        NCH = 9

        ln_state = {"k": 0, "norm": {}, "v1": {}}

        def emit_ln(src_of, W, g_need, dst_of):
            k = ln_state["k"]
            ln_state["k"] = k + 1
            sq3 = lambda: A.v3("sqb%d" % (k % 2), DC, 512)
            @block.scalar
            def _(s):
                s.wait_ge(G_, g_need)
                if k >= 2:
                    s.wait_ge(VE, ln_state["norm"][k - 2])
                sq = sq3()
                for d in range(DC):
                    i = s.activation(sq[:, d, 0:W], src_of(d), AF.Square)
                i.then_inc(SE, 1)
            s_sq = bump("sc")
            @block.tensor
            def _(t):
                t.wait_ge(SE, s_sq)
                if k >= 1:
                    t.wait_ge(VE, ln_state["v1"][k - 1])
                sq = sq3()
                for d in range(DC):
                    t.matmul(PR[0][0:1, 0:W], ones_col[:], src_of(d),
                             start=(d == 0), stop=(d == DC - 1))
                for d in range(DC):
                    i = t.matmul(PR[1][0:1, 0:W], ones_col[:], sq[:, d, 0:W],
                                 start=(d == 0), stop=(d == DC - 1))
                i.then_inc(TMS, 1)
            t_st = bump("t")
            @block.vector
            def _(v):
                v.wait_ge(TMS, t_st)
                m, ms = strow[0:1, 0:W], strow[0:1, 512:512 + W]
                var = strow[0:1, 1024:1024 + W]
                v.tensor_single_scalar(m, PR[0][0:1, 0:W], 1.0 / D, AL.mult)
                v.tensor_single_scalar(ms, PR[1][0:1, 0:W], 1.0 / D, AL.mult)
                v.tensor_tensor(var, m, m, AL.mult)
                v.tensor_sub(var, ms, var).then_inc(VE, 1)
            v1 = bump("v")
            ln_state["v1"][k] = v1
            @block.scalar
            def _(s):
                s.wait_ge(VE, v1)
                s.activation(strow[0:1, 1536:1536 + W], strow[0:1, 1024:1024 + W],
                             AF.Sqrt, bias=eps_col[0:1, 0:1]).then_inc(SE, 1)
            s1 = bump("sc")
            @block.vector
            def _(v):
                v.wait_ge(SE, s1)
                a, b = strow[0:1, 2048:2048 + W], strow[0:1, 2560:2560 + W]
                v.reciprocal(a, strow[0:1, 1536:1536 + W])
                v.scalar_tensor_tensor(b, strow[0:1, 0:W], -1.0, a, AL.mult, AL.mult)
                v.tensor_copy(ab_bf[0:1, 0:W], a)
                v.tensor_copy(ab_bf[0:1, 512:512 + W], b).then_inc(VE, 1)
            v2 = bump("v")
            @block.tensor
            def _(t):
                t.wait_ge(VE, v2)
                t.matmul(PBb[0][:, 0:W], ones_row[:], ab_bf[0:1, 0:W],
                         start=True, stop=True)
                t.matmul(PBb[1][:, 0:W], ones_row[:], ab_bf[0:1, 512:512 + W],
                         start=True, stop=True).then_inc(TMS, 1)
            t2 = bump("t")
            @block.scalar
            def _(s):
                s.wait_ge(TMS, t2)
                s.copy(ABt[:, 0:W], PBb[0][:, 0:W])
                s.copy(ABt[:, 512:512 + W], PBb[1][:, 0:W]).then_inc(SE, 1)
            s2 = bump("sc")
            @block.vector
            def _(v):
                v.wait_ge(SE, s2)
                sq = sq3()
                for d in range(DC):
                    v.tensor_mul(sq[:, d, 0:W], src_of(d), ABt[:, 0:W])
                for d in range(DC):
                    i = v.tensor_add(dst_of(d), sq[:, d, 0:W], ABt[:, 512:512 + W])
                i.then_inc(VE, 1)
            nd = bump("v")
            ln_state["norm"][k] = nd
            return nd

        def raw_of(cx, d):
            if cx == 0:
                return A.v3("xmr", DC, 1026)[:, d, 0:512]
            if cx == 1:
                return A.v3("xmr", DC, 1026)[:, d, 512:1024]
            if cx == 2:
                return A.v3("pxn", DC, 512)[:, d, 0:512]
            return A.v3("xin%d" % (cx % 2), DC, 512)[:, d, 0:512]

        def dst_of(cx, d):
            return raw_of(cx, d) if cx < 3 else A.v3("pxn", DC, 512)[:, d, 0:512]

        stat_id = {}
        g_xin = {}
        p1 = {"norm": {}, "v1": {}, "v2": {}, "bcast": {}, "sq": {},
              "spill": {}, "relu_last": sc_hb0, "g2_last": 0,
              "w2copy": 0, "g1_last": 0}

        for w in range(-2, 9):
            cn, cs = w + 1, w + 2
            gw = {1: 6, 2: 7}.get(w, w - 3 if w >= 3 else None)

            if 0 <= cn < NCH:
                sq_ve = p1["sq"][cn]
                v1_prev = p1["v1"].get(cn - 1, 0)
                @block.tensor
                def _(t, cn=cn, sq_ve=sq_ve, v1_prev=v1_prev):
                    t.wait_ge(VE, sq_ve)
                    if v1_prev:
                        t.wait_ge(VE, v1_prev)
                    sq = A.v3("sqb%d" % (cn % 2), DC, 512)
                    for d in range(DC):
                        t.matmul(PR[0][0:1, 0:512], ones_col[:], raw_of(cn, d),
                                 start=(d == 0), stop=(d == DC - 1))
                    for d in range(DC):
                        i = t.matmul(PR[1][0:1, 0:512], ones_col[:],
                                     sq[:, d, 0:512],
                                     start=(d == 0), stop=(d == DC - 1))
                    i.then_inc(TMS, 1)
                stat_id[cn] = bump("t")

            if w == -2:
                @block.gpsimd
                def _(gp):
                    gp.dma_start(out=A.ap("xin1")[:, 0:DC * 512],
                                 in_=xp_d[0]).then_inc(G_, 16)
                    gp.dma_start(out=A.ap("xin0")[:, 0:DC * 512],
                                 in_=xp_d[1]).then_inc(G_, 16)
                g_xin[3] = bump("g") * 16
                g_xin[4] = bump("g") * 16
            if w == 8:
                A.free("xin0", "xin1", "sqb0", "sqb1")
                A.alloc("wu", DC * 2048)
                @block.gpsimd
                def _(gp):
                    gp.wait_ge(TMS, stat_id[8])
                    gp.wait_ge(VE, p1["norm"][8])
                    gp.dma_start(out=A.v3("wu", DC, 2048),
                                 in_=r3(wuT_d, 0, R)).then_inc(G_, 16)
                g_wu = bump("g") * 16

            if w == 0:
                @block.vector
                def _(v):
                    x3 = A.v3("xmr", DC, 1026)
                    px = A.v3("pxn", DC, 512)
                    for d in range(DC):
                        i = v.tensor_copy(px[:, d, 0:512], x3[:, d, 514:1026])
                    i.then_inc(VE, 1)
                bump("v")
            if w == 2:
                @block.vector
                def _(v):
                    x3 = A.v3("xmr", DC, 1026)
                    px = A.v3("pxn", DC, 512)
                    for d in range(DC):
                        i = v.tensor_copy(x3[:, d, 1024:1026], px[:, d, 510:512])
                    i.then_inc(VE, 1)
                p1["w2copy"] = bump("v")
            if 0 <= cs < NCH:
                gneed = g_xmr if cs <= 1 else (0 if cs == 2 else g_xin[cs])
                st_par = stat_id.get(cs - 2, 0)
                @block.vector
                def _(v, cs=cs, gneed=gneed, st_par=st_par):
                    if gneed:
                        v.wait_ge(G_, gneed)
                    if st_par:
                        v.wait_ge(TMS, st_par)
                    sq = A.v3("sqb%d" % (cs % 2), DC, 512)
                    for d in range(DC):
                        i = v.tensor_tensor(sq[:, d, 0:512], raw_of(cs, d),
                                            raw_of(cs, d), AL.mult)
                    i.then_inc(VE, 1)
                p1["sq"][cs] = bump("v")

            if 0 <= cn < NCH:
                @block.vector
                def _(v, cn=cn):
                    v.wait_ge(TMS, stat_id[cn])
                    m, ms = strow[0:1, 0:512], strow[0:1, 512:1024]
                    var = strow[0:1, 1024:1536]
                    v.tensor_single_scalar(m, PR[0][0:1, 0:512], 1.0 / D, AL.mult)
                    v.tensor_single_scalar(ms, PR[1][0:1, 0:512], 1.0 / D, AL.mult)
                    v.tensor_tensor(var, m, m, AL.mult)
                    v.tensor_sub(var, ms, var).then_inc(VE, 1)
                p1["v1"][cn] = bump("v")
                @block.scalar
                def _(s, v1=p1["v1"][cn]):
                    s.wait_ge(VE, v1)
                    s.activation(strow[0:1, 1536:2048], strow[0:1, 1024:1536],
                                 AF.Sqrt, bias=eps_col[0:1, 0:1]).then_inc(SE, 1)
                s1 = bump("sc")
                bc_prev = p1["bcast"].get(cn - 1, 0)
                @block.vector
                def _(v, s1=s1, bc_prev=bc_prev):
                    v.wait_ge(SE, s1)
                    if bc_prev:
                        v.wait_ge(TMS, bc_prev)
                    a, b = strow[0:1, 2048:2560], strow[0:1, 2560:3072]
                    v.reciprocal(a, strow[0:1, 1536:2048])
                    v.scalar_tensor_tensor(b, strow[0:1, 0:512], -1.0, a,
                                           AL.mult, AL.mult)
                    v.tensor_copy(ab_bf[0:1, 0:512], a)
                    v.tensor_copy(ab_bf[0:1, 512:1024], b).then_inc(VE, 1)
                p1["v2"][cn] = bump("v")

            if gw is not None:
                if gw == 6:
                    xn_of = lambda d: A.v3("xmr", DC, 1026)[:, d, 1:513]
                    v_need = p1["norm"][1]
                elif gw == 7:
                    xn_of = lambda d: A.v3("xmr", DC, 1026)[:, d, 513:1025]
                    v_need = p1["w2copy"]
                else:
                    xn_of = lambda d: A.v3("pxn", DC, 512)[:, d, 0:512]
                    v_need = p1["norm"][gw + 3]
                relu_base = c["sc"]
                scp = p1["relu_last"]
                t_g1 = [c["t"] + 1 + h for h in range(HC)]
                @block.tensor
                def _(t, gw=gw, xn_of=xn_of, v_need=v_need,
                      relu_base=relu_base, scp=scp):
                    t.wait_ge(VE, v_need)
                    if gw == 6:
                        t.wait_ge(G_, g_wg)
                    t.wait_ge(SE, scp)
                    wg1 = A.v3("wg1", DC, 512)
                    for h in range(HC):
                        if h >= 2:
                            t.wait_ge(SE, relu_base + h - 1)
                        for d in range(DC):
                            i = t.matmul(PM[h % 2][:, 0:512],
                                         wg1[:, d, h * 128:(h + 1) * 128],
                                         xn_of(d), start=(d == 0),
                                         stop=(d == DC - 1))
                        i.then_inc(TMS, 1)
                bump("t", HC)
                p1["g1_last"] = t_g1[-1]

            if 0 <= cn < NCH:
                @block.tensor
                def _(t, v2=p1["v2"][cn]):
                    t.wait_ge(VE, v2)
                    t.matmul(PBb[0][:, 0:512], ones_row[:], ab_bf[0:1, 0:512],
                             start=True, stop=True)
                    t.matmul(PBb[1][:, 0:512], ones_row[:],
                             ab_bf[0:1, 512:1024], start=True,
                             stop=True).then_inc(TMS, 1)
                p1["bcast"][cn] = bump("t")

            if gw is not None:
                g2_prev = p1["g2_last"]
                @block.scalar
                def _(s, t_g1=t_g1, g2_prev=g2_prev):
                    if g2_prev:
                        s.wait_ge(TMS, g2_prev)
                    for h in range(HC):
                        s.wait_ge(TMS, t_g1[h])
                        s.activation(hh[:, h, 0:512], PM[h % 2][:, 0:512],
                                     AF.Relu,
                                     bias=bcols[:, B_BG1 + h:B_BG1 + h + 1]
                                     ).then_inc(SE, 1)
                bump("sc", HC)
                relu_last = c["sc"]
                assert relu_last == relu_base + HC
                p1["relu_last"] = relu_last
                t_g2 = [c["t"] + 1 + r for r in range(RC)]
                @block.tensor
                def _(t, relu_last=relu_last):
                    wg2 = A.v3("wg2", HC, 2048)
                    for r in range(RC):
                        t.wait_ge(SE, relu_last if r < 2 else relu_last + r - 1)
                        for h in range(HC):
                            i = t.matmul(PM[2 + r % 2][:, 0:512],
                                         wg2[:, h, r * 128:(r + 1) * 128],
                                         hh[:, h, 0:512], start=(h == 0),
                                         stop=(h == HC - 1))
                        i.then_inc(TMS, 1)
                bump("t", RC)
                p1["g2_last"] = t_g2[-1]
                sig_base = c["sc"]
                sd_g = c["s"]
                @block.scalar
                def _(s, t_g2=t_g2, sd_g=sd_g):
                    for r in range(RC):
                        s.wait_ge(TMS, t_g2[r])
                        if r >= 2:
                            s.wait_ge(SD, (sd_g + r - 1) * 16)
                        s.activation(A.ap("gth%d" % (r % 2)),
                                     PM[2 + r % 2][:, 0:512], AF.Sigmoid,
                                     bias=bcols[:, B_BG2 + r:B_BG2 + r + 1]
                                     ).then_inc(SE, 1)
                bump("sc", RC)
                @block.sync
                def _(sy, gw=gw, sig_base=sig_base):
                    for r in range(RC):
                        sy.wait_ge(SE, sig_base + r + 1)
                        sy.dma_start(out=gt_sp[gw][:, r * 512:(r + 1) * 512],
                                     in_=A.ap("gth%d" % (r % 2))).then_inc(SD, 16)
                bump("s", RC)

            if 0 <= cn < NCH:
                g1_gate = p1["g1_last"] if (gw is not None and cn >= 4) else 0
                sp_prev = p1["spill"].get(cn - 1, 0)
                @block.vector
                def _(v, cn=cn, g1_gate=g1_gate, sp_prev=sp_prev,
                      bc=p1["bcast"][cn]):
                    v.wait_ge(TMS, bc)
                    v.tensor_single_scalar(ABt[:, 0:512], PBb[0][:, 0:512],
                                           1.0, AL.mult)
                    v.tensor_single_scalar(ABt[:, 512:1024], PBb[1][:, 0:512],
                                           1.0, AL.mult)
                    if g1_gate:
                        v.wait_ge(TMS, g1_gate)
                    if sp_prev:
                        v.wait_ge(SD, sp_prev)
                    for d in range(DC):
                        v.tensor_mul(dst_of(cn, d), raw_of(cn, d), ABt[:, 0:512])
                    for d in range(DC):
                        i = v.tensor_add(dst_of(cn, d), dst_of(cn, d),
                                         ABt[:, 512:1024])
                    i.then_inc(VE, 1)
                p1["norm"][cn] = bump("v")
                if cn == 1:
                    @block.sync
                    def _(sy, vn=p1["norm"][1]):
                        sy.wait_ge(VE, vn)
                        sy.dma_start(out=xn_sp[6],
                                     in_=A.v3("xmr", DC, 1026)[:, :, 1:513]
                                     ).then_inc(SD, 16)
                    bump("s")
                elif cn >= 3:
                    @block.sync
                    def _(sy, vn=p1["norm"][cn], k=cn - 3):
                        sy.wait_ge(VE, vn)
                        sy.dma_start(out=xn_sp[k],
                                     in_=A.ap("pxn")[:, 0:DC * 512]
                                     ).then_inc(SD, 16)
                    p1["spill"][cn] = bump("s") * 16
            if w == 2:
                @block.sync
                def _(sy, w2=p1["w2copy"]):
                    sy.wait_ge(VE, w2)
                    sy.dma_start(out=xn_sp[7],
                                 in_=A.v3("xmr", DC, 1026)[:, :, 513:1025]
                                 ).then_inc(SD, 16)
                bump("s")

            if 2 <= w <= 5:
                @block.gpsimd
                def _(gp, k=w):
                    gp.wait_ge(TMS, stat_id[k + 1])
                    gp.wait_ge(VE, p1["norm"][k + 1])
                    gp.dma_start(out=A.ap("xin%d" % ((k + 3) % 2))[:, 0:DC * 512],
                                 in_=xp_d[k]).then_inc(G_, 16)
                g_xin[w + 3] = bump("g") * 16

        A.free("pxn")
        A.alloc("cvf", 2 * 1024)
        A.alloc("cvo", 4 * 1024)
        v_conv0 = c["v"]
        s_cv0 = c["s"]
        @block.vector
        def _(v):
            v.wait_ge(TMS, p1["g1_last"])
            v.wait_ge(SD, p1["spill"][8])
            x3 = A.v3("xmr", DC, 1026)
            cvf = A.f32("cvf")[:, 0:1024]
            cvo = A.v3("cvo", 4, 1024)
            for d in range(DC):
                if d >= 4:
                    v.wait_ge(SD, (s_cv0 + 2 * (d - 3)) * 16)
                v.tensor_scalar(cvf, x3[:, d, 0:1024],
                                bcols[:, B_CW + d:B_CW + d + 1],
                                bcols[:, B_CB + d:B_CB + d + 1], AL.mult, AL.add)
                v.scalar_tensor_tensor(cvf, x3[:, d, 1:1025],
                                       bcols[:, B_CW + 16 + d:B_CW + 17 + d],
                                       cvf, AL.mult, AL.add)
                v.scalar_tensor_tensor(cvo[:, d % 4, 0:1024], x3[:, d, 2:1026],
                                       bcols[:, B_CW + 32 + d:B_CW + 33 + d],
                                       cvf, AL.mult, AL.add).then_inc(VE, 1)
        bump("v", DC)
        @block.sync
        def _(sy):
            cvo = A.v3("cvo", 4, 1024)
            for d in range(DC):
                sy.wait_ge(VE, v_conv0 + d + 1)
                sy.dma_start(out=cv_sp[0][:, d * 512:(d + 1) * 512],
                             in_=cvo[:, d % 4, 0:512]).then_inc(SD, 16)
                sy.dma_start(out=cv_sp[1][:, d * 512:(d + 1) * 512],
                             in_=cvo[:, d % 4, 512:1024]).then_inc(SD, 16)
        bump("s", 2 * DC)

        barrier()
        A.free("xmr", "cvf", "cvo", "gth0", "gth1", "wg1", "wg2")

        A.alloc("xrc0", DC * 512)
        A.alloc("xrc1", DC * 512)
        A.alloc("gtc0", DC * 512)
        A.alloc("gtc1", DC * 512)
        A.alloc("kcc", 2 * 1024)

        @block.sync
        def _(sy):
            sy.wait_ge(G_, g_wu)
            sy.dma_start(out=wub_sp.rearrange("(c p) r -> p c r", p=128),
                         in_=A.v3("wu", DC, 2048)).then_inc(SD, 16)
        bump("s")

        vend = {}
        for hidx, (kind, j) in enumerate([("p", k) for k in range(6)] + [("m", 0), ("m", 1)]):
            xa = j if kind == "p" else 6 + j
            pv = vend.get(hidx - 2, 0)
            xrn, gtn = "xrc%d" % (hidx % 2), "gtc%d" % (hidx % 2)
            @block.sync
            def _(sy, xa=xa, pv=pv, xrn=xrn, gtn=gtn):
                sy.wait_ge(VE, pv)
                sy.dma_start(out=A.ap(xrn)[:, 0:DC * 512],
                             in_=xn_sp[xa]).then_inc(SD, 16)
                sy.dma_start(out=A.ap(gtn)[:, 0:DC * 512],
                             in_=gt_sp[xa]).then_inc(SD, 16)
            s_ld = bump("s", 2) * 16

            def t_emit(t, i, s_ld=s_ld, xrn=xrn):
                if i == 0:
                    t.wait_ge(SD, s_ld)
                wu = A.v3("wu", DC, 2048)
                xr = A.v3(xrn, DC, 512)
                for d in range(DC):
                    mm = t.matmul(PM[i % 4][:, 0:512], wu[:, d, i * 128:(i + 1) * 128],
                                  xr[:, d, 0:512], start=(d == 0), stop=(d == DC - 1))
                mm.then_inc(TMS, 1)

            if kind == "p":
                def v_emit(v, i, hidx=hidx, gtn=gtn):
                    gt = A.v3(gtn, DC, 512)
                    v.scalar_tensor_tensor(ugv[:], PM[i % 4][:, 0:512],
                                           bcols[:, B_BU + i:B_BU + i + 1],
                                           gt[:, i, 0:512], AL.add, AL.mult)
                    v.reduce_sum(acols[:, hidx * 16 + i:hidx * 16 + i + 1],
                                 ugv[:], AXX).then_inc(VE, 1)
                pipe(RC, t_emit, "vector", v_emit, depth=4)
            else:
                s0_kc = c["s"]
                v0 = c["v"]
                def v_emit(v, i, s0_kc=s0_kc, v0=v0, gtn=gtn):
                    gt = A.v3(gtn, DC, 512)
                    v.wait_ge(SD, (s0_kc + i - 1) * 16 if i >= 2 else s0_kc * 16)
                    v.scalar_tensor_tensor(ugv[:], PM[i % 4][:, 0:512],
                                           bcols[:, B_BU + i:B_BU + i + 1],
                                           gt[:, i, 0:512], AL.add, AL.mult)
                    v.tensor_tensor_scan(kcsb[i % 2][:], ugv[:], ugv[:],
                                         kl_col[:, i:i + 1], AL.add,
                                         AL.bypass).then_inc(VE, 1)
                    v.wait_ge(VE, v0 + 2 * i + 1)
                    v.tensor_copy(kl_col[:, i:i + 1],
                                  kcsb[i % 2][:, 511:512]).then_inc(VE, 1)
                pipe(RC, t_emit, "vector", v_emit, e_incs=2, depth=4)
                @block.sync
                def _(sy, j=j, v0=v0):
                    for i in range(RC):
                        sy.wait_ge(VE, v0 + 2 * i + 2)
                        sy.dma_start(out=kc_sp[i, j],
                                     in_=kcsb[i % 2][:]).then_inc(SD, 16)
                bump("s", RC)
            vend[hidx] = c["v"]
        t_p2e = c["t"]

        vsz0 = c["v"]
        @block.vector
        def _(v):
            a3 = acols[:].rearrange("p (h i) -> p h i", i=16)
            v.tensor_add(srun[:], a3[:, 0, :], a3[:, 1, :]).then_inc(VE, 1)
            v.wait_ge(VE, vsz0 + 1)
            for h in range(2, 6):
                v.tensor_add(srun[:], srun[:], a3[:, h, :]).then_inc(VE, 1)
                v.wait_ge(VE, vsz0 + h)
            v.scalar_tensor_tensor(sza[:, 0:16], ccol[:], npz[:, 0:1], srun[:],
                                   AL.mult, AL.add).then_inc(VE, 1)
            v.tensor_copy(sza[:, 16:32], kl_col[:]).then_inc(VE, 1)
        v_sz = bump("v", 7)

        A.free("wu")
        A.alloc("wv", DC * 2048)
        A.alloc("Gh", RC * 512)

        @block.gpsimd
        def _(gp):
            gp.wait_ge(TMS, t_p2e)
            gp.dma_start(out=A.v3("wv", DC, 2048), in_=r3(wvT_d, 0, R)).then_inc(G_, 16)
        g_wv = bump("g") * 16

        @block.gpsimd
        def _(gp):
            gp.wait_ge(VE, v_sz)
            pid = gp.partition_id()
            with gp.register("role") as role:
                gp.reg_alu(role, pid, 1, AL.bitwise_and)
                with gp.If_eq(role, 0):
                    gp.dma_start(out=xch[0, 0], in_=sza[:, 0:16]).then_inc(G_, 16)
                    gp.dma_start(out=xch[0, 1], in_=sza[:, 16:32]).then_inc(G_, 16)
                with gp.Else():
                    gp.dma_start(out=xch[1, 0], in_=sza[:, 0:16]).then_inc(G_, 16)
                    gp.dma_start(out=xch[1, 1], in_=sza[:, 16:32]).then_inc(G_, 16)
                gp.wait_ge(G_, bump("g", 2) * 16)
                with gp.If_eq(role, 0):
                    gp.dma_start(out=xfl[0:1, :], in_=nonce_t[:]).then_inc(G_, 16)
                with gp.Else():
                    gp.dma_start(out=xfl[1:2, :], in_=nonce_t[:]).then_inc(G_, 16)
                gp.wait_ge(G_, bump("g") * 16)
                with gp.register("cntr") as cntr, gp.register("f0") as f0, \
                        gp.register("f1") as f1, gp.register("nd") as nd, \
                        gp.register("nd2") as nd2:
                    gp.reg_mov(cntr, 0)
                    gp.reg_mov(nd, 1)
                    with gp.While(nd):
                        with gp.If_eq(role, 0):
                            gp.dma_start(out=flag_t[:], in_=xfl[1:2, :]).then_inc(PXs, 16)
                        with gp.Else():
                            gp.dma_start(out=flag_t[:], in_=xfl[0:1, :]).then_inc(PXs, 16)
                        gp.reg_add(cntr, cntr, 16)
                        gp.wait_ge(PXs, cntr)
                        gp.load(f0, flag_t[0:1, 0:1])
                        gp.load(f1, flag_t[0:1, 1:2])
                        gp.reg_alu(nd, f0, nonce0, AL.not_equal)
                        gp.reg_alu(nd2, f1, nonce1, AL.not_equal)
                        gp.reg_alu(nd, nd, nd2, AL.bitwise_or)
                    with gp.If_eq(role, 0):
                        gp.dma_start(out=sza[:, 32:48], in_=xch[1, 0]).then_inc(G_, 16)
                        gp.dma_start(out=sza[:, 48:64], in_=xch[1, 1]).then_inc(G_, 16)
                    with gp.Else():
                        gp.dma_start(out=sza[:, 32:48], in_=xch[0, 0]).then_inc(G_, 16)
                        gp.dma_start(out=sza[:, 48:64], in_=xch[0, 1]).then_inc(G_, 16)
                    gp.wait_ge(G_, bump("g", 2) * 16)
                    gp.sem_inc(FB, 1)
        fb_x = bump("fb")

        voff0 = c["v"]
        @block.vector
        def _(v):
            v.wait_ge(FB, fb_x)
            v.tensor_add(offc[:], sza[:, 0:16], sza[:, 32:48]).then_inc(VE, 1)
            v.wait_ge(VE, voff0 + 1)
            v.scalar_tensor_tensor(offc[:], sza[:, 48:64], npz[:, 1:2], offc[:],
                                   AL.mult, AL.add).then_inc(VE, 1)
        bump("v", 2)

        if debug:
            vv = c["v"]
            @block.sync
            def _(sy):
                sy.wait_ge(VE, vv)
                sy.dma_start(out=dbg["off"][:], in_=offc[:]).then_inc(SD, 16)
                sy.dma_start(out=dbg["srun"][:], in_=srun[:]).then_inc(SD, 16)
                sy.dma_start(out=dbg["ccol"][:], in_=ccol[:]).then_inc(SD, 16)
            bump("s", 3)

        s_gsp = [0]
        for j in range(2):
            hidx = 8 + j
            pv = vend[hidx - 2]
            xrn, gtn = "xrc%d" % (hidx % 2), "gtc%d" % (hidx % 2)
            @block.sync
            def _(sy, j=j, pv=pv, xrn=xrn, gtn=gtn):
                sy.wait_ge(VE, pv)
                sy.dma_start(out=A.ap(xrn)[:, 0:DC * 512],
                             in_=xn_sp[6 + j]).then_inc(SD, 16)
                sy.dma_start(out=A.ap(gtn)[:, 0:DC * 512],
                             in_=gt_sp[6 + j]).then_inc(SD, 16)
            s_ld = bump("s", 2) * 16
            s_kc0 = c["s"]
            v0j = c["v"]
            @block.sync
            def _(sy, j=j, v0j=v0j, s_kc0=s_kc0):
                kcv = A.f32("kcc").rearrange("p (c t) -> p c t", t=512)
                for i in range(RC):
                    sy.wait_ge(VE, v0j + i - 1 if i >= 2 else v0j)
                    sy.dma_start(out=kcv[:, i % 2, 0:512],
                                 in_=kc_sp[i, j]).then_inc(SD, 16)
            bump("s", RC)

            def t_emit(t, i, s_ld=s_ld, j=j, xrn=xrn):
                if i == 0:
                    t.wait_ge(SD, s_ld)
                    if j == 0:
                        t.wait_ge(G_, g_wv)
                wv = A.v3("wv", DC, 2048)
                xr = A.v3(xrn, DC, 512)
                for d in range(DC):
                    mm = t.matmul(PM[i % 4][:, 0:512], wv[:, d, i * 128:(i + 1) * 128],
                                  xr[:, d, 0:512], start=(d == 0), stop=(d == DC - 1))
                mm.then_inc(TMS, 1)

            def v_emit(v, i, j=j, s_kc0=s_kc0, gtn=gtn):
                gt = A.v3(gtn, DC, 512)
                kcv = A.f32("kcc").rearrange("p (c t) -> p c t", t=512)
                Gv = A.v3("Gh", RC, 512)
                v.wait_ge(SD, (s_kc0 + i + 1) * 16)
                if j == 1 and i == 0:
                    v.wait_ge(SD, s_gsp[0])
                v.scalar_tensor_tensor(ugv[:], PM[i % 4][:, 0:512],
                                       bcols[:, B_BV + i:B_BV + i + 1],
                                       gt[:, i, 0:512], AL.add, AL.mult)
                v.scalar_tensor_tensor(Gv[:, i, 0:512],
                                       kcv[:, i % 2, 0:512], offc[:, i:i + 1],
                                       ugv[:], AL.add, AL.mult).then_inc(VE, 1)
            pipe(RC, t_emit, "vector", v_emit, depth=4)
            if j == 0:
                ve_h0 = c["v"]
                @block.sync
                def _(sy, ve_h0=ve_h0):
                    sy.wait_ge(VE, ve_h0)
                    sy.dma_start(out=g_sp[:], in_=A.ap("Gh")[:, 0:RC * 512]).then_inc(SD, 16)
                s_gsp[0] = bump("s") * 16

        barrier()
        A.free("wv", "gtc0", "gtc1", "kcc", "xrc0", "xrc1")

        A.alloc("wub", RC * 2048)
        A.alloc("cvc", DC * 512)
        A.alloc("attn", DC * 1024)

        s_wub = (c["s"] + 1) * 16
        @block.sync
        def _(sy):
            sy.dma_start(out=A.ap("wub")[:, 0:RC * 2048],
                         in_=wub_sp.rearrange("(r p) d -> p r d", p=128)).then_inc(SD, 16)
        bump("s")

        for j in (1, 0):
            v_prev = c["v"]
            t_prev = c["t"]
            @block.sync
            def _(sy, j=j, v_prev=v_prev, t_prev=t_prev):
                sy.wait_ge(VE, v_prev)
                sy.dma_start(out=A.ap("cvc")[:, 0:DC * 512],
                             in_=cv_sp[j]).then_inc(SD, 16)
                if j == 0:
                    sy.wait_ge(TMS, t_prev)
                    sy.dma_start(out=A.ap("Gh")[:, 0:RC * 512],
                                 in_=g_sp[:]).then_inc(SD, 16)
            s_cv = bump("s", 1 if j == 1 else 2) * 16
            for gix in range(4):
                v_evac_prev = c["v"]
                @block.tensor
                def _(t, gix=gix, j=j, s_cv=s_cv, v_evac_prev=v_evac_prev):
                    t.wait_ge(VE, v_evac_prev)
                    if gix == 0:
                        t.wait_ge(SD, s_cv)
                        t.wait_ge(SD, s_wub)
                    wubv = A.ap("wub")[:, 0:RC * 2048].rearrange(
                        "p (r d) -> p r d", d=2048)
                    Gv = A.v3("Gh", RC, 512)
                    for dd in range(4):
                        d = gix * 4 + dd
                        for rr in range(RC):
                            mm = t.matmul(PM[dd][:, 0:512],
                                          wubv[:, rr, d * 128:(d + 1) * 128],
                                          Gv[:, rr, 0:512],
                                          start=(rr == 0), stop=(rr == RC - 1))
                        mm.then_inc(TMS, 1)
                bump("t", 4)
                t_done = c["t"]
                @block.vector
                def _(v, gix=gix, j=j, t_done=t_done, s_cv=s_cv):
                    v.wait_ge(TMS, t_done)
                    v.wait_ge(SD, s_cv)
                    at = A.v3("attn", DC, 1024)
                    cv = A.v3("cvc", DC, 512)
                    for dd in range(4):
                        i = v.tensor_add(at[:, gix * 4 + dd, j * 512:(j + 1) * 512],
                                         PM[dd][:, 0:512], cv[:, gix * 4 + dd, 0:512])
                    i.then_inc(VE, 1)
                bump("v")

        barrier()
        if debug:
            sd_at = c["s"]
            @block.sync
            def _(sy):
                sy.dma_start(out=dbg["atn"][:],
                             in_=A.ap("attn")[:, 0:DC * 1024]).then_inc(SD, 16)
            bump("s")
            barrier()
        A.free("wub", "cvc", "Gh")

        A.alloc("sqb0", DC * 512)
        A.alloc("sqb1", DC * 512)
        A.alloc("xbm", DC * 1024)

        @block.gpsimd
        def _(gp):
            gp.dma_start(out=A.v3("xbm", DC, 1024),
                         in_=xm_d[:, :, 1:1025]).then_inc(G_, 16)
        g_xb = bump("g") * 16

        at3 = lambda: A.v3("attn", DC, 1024)
        for j in range(2):
            v_n = emit_ln(lambda d, j=j: at3()[:, d, j * 512:(j + 1) * 512],
                          512, 0, lambda d, j=j: at3()[:, d, j * 512:(j + 1) * 512])
            @block.vector
            def _(v, j=j, v_n=v_n):
                v.wait_ge(VE, v_n)
                v.wait_ge(G_, g_xb)
                at = at3()
                xb3 = A.v3("xbm", DC, 1024)
                for d in range(DC):
                    i = v.tensor_add(at[:, d, j * 512:(j + 1) * 512],
                                     at[:, d, j * 512:(j + 1) * 512],
                                     xb3[:, d, j * 512:(j + 1) * 512])
                i.then_inc(VE, 1)
            bump("v")

        v_x1 = c["v"]
        @block.sync
        def _(sy):
            sy.wait_ge(VE, v_x1)
            sy.dma_start(out=x1_sp[:], in_=A.ap("attn")[:, 0:DC * 1024]).then_inc(SD, 16)
        bump("s")

        for j in range(2):
            emit_ln(lambda d, j=j: at3()[:, d, j * 512:(j + 1) * 512], 512, 0,
                    lambda d, j=j: A.v3("xbm", DC, 1024)[:, d, j * 512:(j + 1) * 512])

        barrier()
        A.free("attn", "sqb0", "sqb1")
        A.rename("xbm", "xn2")

        A.alloc("gtw", DC * 8)

        @block.gpsimd
        def _(gp):
            gp.dma_start(out=A.v3("gtw", DC, 8),
                         in_=gateT_d.rearrange("(c p) e -> p c e", p=128)).then_inc(G_, 16)
            gp.dma_start(out=buRb[:].rearrange("e (c d) -> e c d", d=128),
                         in_=buR_d.rearrange("c e d -> e c d")).then_inc(G_, 16)
        g_gt = bump("g", 2) * 16

        xn23 = lambda: A.v3("xn2", DC, 1024)
        sc_lg0 = c["sc"]
        t_lg0 = c["t"]
        @block.tensor
        def _(t):
            t.wait_ge(G_, g_gt)
            gtw = A.v3("gtw", DC, 8)
            for j in range(2):
                if j == 1:
                    t.wait_ge(SE, sc_lg0 + 1)
                for d in range(DC):
                    i = t.matmul(PM[2][0:8, 0:512], gtw[:, d, 0:8],
                                 xn23()[:, d, j * 512:(j + 1) * 512],
                                 start=(d == 0), stop=(d == DC - 1))
                i.then_inc(TMS, 1)
        bump("t", 2)
        @block.scalar
        def _(s):
            for j in range(2):
                s.wait_ge(TMS, t_lg0 + j + 1)
                s.copy(lgr[:, j * 512:(j + 1) * 512], PM[2][0:8, 0:512]).then_inc(SE, 1)
        sc_lg = bump("sc", 2)

        def t_tr(t, q):
            if q == 0:
                t.wait_ge(SE, sc_lg)
            t.transpose(PM[q % 2][:, 0:8], lgr[:, q * 128:(q + 1) * 128],
                        idf[0:8, 0:8]).then_inc(TMS, 1)

        def s_tr(s, q):
            s.copy(lgm[:, q, :], PM[q % 2][:, 0:8]).then_inc(SE, 1)
        pipe(8, t_tr, "scalar", s_tr)
        sc_tra = c["sc"]

        vg0 = c["v"]
        @block.vector
        def _(v):
            v.wait_ge(SE, sc_tra)
            for q in range(8):
                i = v.reduce_max(m1a[:, q:q + 1], lgm[:, q, :], AXX)
            i.then_inc(VE, 1)
            v.wait_ge(VE, vg0 + 1)
            for q in range(8):
                v.tensor_scalar(cmt[:, q, :], lgm[:, q, :], m1a[:, q:q + 1],
                                None, AL.subtract)
            for q in range(8):
                v.tensor_single_scalar(qxp[:, q, :], cmt[:, q, :], 0.0, AL.is_ge)
            for q in range(8):
                v.scalar_tensor_tensor(qxp[:, q, :], qxp[:, q, :], -1e30,
                                       cmt[:, q, :], AL.mult, AL.add)
            for q in range(8):
                i = v.reduce_max(m2a[:, q:q + 1], qxp[:, q, :], AXX)
            i.then_inc(VE, 1)
        v_cm1 = bump("v", 2)

        @block.scalar
        def _(s):
            s.wait_ge(VE, v_cm1)
            for q in range(8):
                s.activation(qxp[:, q, :], cmt[:, q, :], AF.Exp)
            s.activation(e2a[:], m2a[:], AF.Exp).then_inc(SE, 1)
        sc_cm = bump("sc")

        @block.vector
        def _(v):
            v.wait_ge(SE, sc_cm)
            v.tensor_single_scalar(gsc[:], e2a[:], 1.0, AL.add)
            for q in range(8):
                v.tensor_scalar(lgm[:, q, :], cmt[:, q, :], m2a[:, q:q + 1],
                                None, AL.is_ge)
            v.reciprocal(e2a[:], gsc[:])
            for q in range(8):
                v.tensor_tensor(qxp[:, q, :], qxp[:, q, :], lgm[:, q, :], AL.mult)
            for q in range(8):
                i = v.tensor_scalar(cmt[:, q, :], qxp[:, q, :], e2a[:, q:q + 1],
                                    None, AL.mult)
            i.then_inc(VE, 1)
        v_cm2 = bump("v")

        def t_trb(t, q):
            if q == 0:
                t.wait_ge(VE, v_cm2)
            t.transpose(PM[q % 2][0:8, 0:128], cmt[:, q, :], idf[0:128, 0:128]).then_inc(TMS, 1)

        def s_trb(s, q):
            s.copy(cmr[:, q * 128:(q + 1) * 128], PM[q % 2][0:8, 0:128]).then_inc(SE, 1)
        pipe(8, t_trb, "scalar", s_trb)
        sc_trb = c["sc"]

        @block.vector
        def _(v):
            v.wait_ge(SE, sc_trb)
            v.tensor_copy(cmb[:], cmr[:]).then_inc(VE, 1)
        v_cmb = bump("v")

        if debug:
            @block.sync
            def _(sy):
                sy.wait_ge(VE, v_cmb)
                sy.dma_start(out=dbg["comb"][:], in_=cmr[:]).then_inc(SD, 16)
            bump("s")

        barrier()

        A.free("gtw")
        A.alloc("hTa", E * BC * 512)
        A.alloc("hTb", E * BC * 512)
        A.alloc("sh", BC * 1024)
        A.alloc("wda", DC * 512)
        A.alloc("wdb", DC * 512)
        A.alloc("Bb", 2 * 1024)

        def hT_ap(k):
            if k < 16:
                return A.v3("hTa", 16, 1024)[:, k, :]
            return A.v3("hTb", 16, 1024)[:, k - 16, :]

        g_wd = []
        t_p8 = c["t"]
        @block.gpsimd
        def _(gp):
            for e_ in range(E + 1):
                if e_ >= 2:
                    gp.wait_ge(TMS, t_p8 + 9 * (e_ - 1) - (1 if e_ == E + 0 and False else 0))
                gp.dma_start(out=A.v3("wda" if e_ % 2 == 0 else "wdb", DC, 512),
                             in_=wdT_d[e_].rearrange("(c p) b -> p c b", p=128)).then_inc(G_, 16)
                g_wd.append(bump("g") * 16)

        vend = [0, 0]
        for e_ in range(E + 1):
            slab = "wda" if e_ % 2 == 0 else "wdb"
            is_sh = (e_ == E)
            t0 = c["t"]
            v_base = c["v"]
            sc_base = c["sc"]
            @block.tensor
            def _(t, e_=e_, slab=slab, sc_base=sc_base, is_sh=is_sh):
                t.wait_ge(G_, g_wd[e_])
                t.wait_ge(SE, sc_base)
                if not is_sh:
                    t.matmul(PBb[0][:, 0:512], oneh[0:8, e_ * 128:(e_ + 1) * 128],
                             cmb[0:8, 0:512], start=True, stop=True)
                    t.matmul(PBb[1][:, 0:512], oneh[0:8, e_ * 128:(e_ + 1) * 128],
                             cmb[0:8, 512:1024], start=True,
                             stop=True).then_inc(TMS, 1)
                wd = A.v3(slab, DC, 512)
                for i in range(8):
                    j, bc = i // 4, i % 4
                    if i >= 4:
                        t.wait_ge(SE, sc_base + i - 2)
                    for d in range(DC):
                        mm = t.matmul(PM[bc][:, 0:512],
                                      wd[:, d, bc * 128:(bc + 1) * 128],
                                      xn23()[:, d, j * 512:(j + 1) * 512],
                                      start=(d == 0), stop=(d == DC - 1))
                    mm.then_inc(TMS, 1)
            bump("t", 8 if is_sh else 9)
            @block.scalar
            def _(s, e_=e_, t0=t0, is_sh=is_sh, v_base=v_base):
                s.wait_ge(VE, v_base)
                if not is_sh:
                    s.wait_ge(TMS, t0 + 1)
                    s.wait_ge(VE, vend[0])
                    Bb3 = A.v3("Bb", 2, 1024)
                    s.copy(Bb3[:, e_ % 2, 0:512], PBb[0][:, 0:512])
                    s.copy(Bb3[:, e_ % 2, 512:1024], PBb[1][:, 0:512]).then_inc(SE, 1)
                tb = 1 if not is_sh else 0
                for i in range(8):
                    j, bc = i // 4, i % 4
                    s.wait_ge(TMS, t0 + tb + i + 1)
                    if is_sh:
                        s.activation(A.v3("sh", BC, 1024)[:, bc, j * 512:(j + 1) * 512],
                                     PM[bc][:, 0:512], AF.Gelu,
                                     bias=bcols[:, B_BSD + bc:B_BSD + bc + 1]).then_inc(SE, 1)
                    else:
                        if i >= 2:
                            s.wait_ge(VE, v_base + i - 1)
                        s.activation(gelb[i % 2][:], PM[bc][:, 0:512], AF.Gelu,
                                     bias=bcols[:, B_BD + e_ * 4 + bc:
                                                B_BD + e_ * 4 + bc + 1]).then_inc(SE, 1)
            bump("sc", 8 if is_sh else 9)
            if not is_sh:
                sc_g0 = c["sc"] - 8
                @block.vector
                def _(v, e_=e_, sc_g0=sc_g0):
                    for i in range(8):
                        j, bc = i // 4, i % 4
                        v.wait_ge(SE, sc_g0 + i + 1)
                        v.tensor_mul(hT_ap(e_ * BC + bc)[:, j * 512:(j + 1) * 512],
                                     gelb[i % 2][:],
                                     A.v3("Bb", 2, 1024)[:, e_ % 2,
                                                         j * 512:(j + 1) * 512]
                                     ).then_inc(VE, 1)
                bump("v", 8)
            vend = [vend[1], c["v"]]

        barrier()
        A.free("wda", "wdb", "Bb", "xn2")

        A.alloc("wt0", 36 * 128)
        A.alloc("wt1", 36 * 128)
        A.alloc("ffna", 8 * 1024)
        A.alloc("ffnb", 8 * 1024)
        A.alloc("sq9", 1024)

        def ffn_ap(dc):
            if dc < 8:
                return A.v3("ffna", 8, 1024)[:, dc, :]
            return A.v3("ffnb", 8, 1024)[:, dc - 8, :]

        g_wt = []
        t_p9 = c["t"]
        @block.gpsimd
        def _(gp):
            for dc in range(DC):
                if dc >= 2:
                    gp.wait_ge(TMS, t_p9 + 3 * (dc - 2) + 2)
                w = "wt%d" % (dc % 2)
                gp.dma_start(out=A.ap(w)[:, 0:36 * 128],
                             in_=wuPM_d[dc]).then_inc(G_, 16)
                g_wt.append(bump("g") * 16)

        sc_p9 = c["sc"]
        for dc in range(DC):
            w = "wt%d" % (dc % 2)
            t0 = c["t"]
            @block.tensor
            def _(t, dc=dc, w=w):
                t.wait_ge(G_, g_wt[dc])
                if dc == 0:
                    t.wait_ge(G_, g_gt)
                wt = A.v3(w, 36, 128)
                sh = A.v3("sh", BC, 1024)
                for j in range(2):
                    if dc > 0:
                        t.wait_ge(SE, sc_p9 + 2 * (dc - 1) + j + 1)
                    for k in range(32):
                        t.matmul(PM[j][:, 0:512], wt[:, k, :],
                                 hT_ap(k)[:, j * 512:(j + 1) * 512],
                                 start=(k == 0), stop=False)
                    for bcx in range(4):
                        t.matmul(PM[j][:, 0:512], wt[:, 32 + bcx, :],
                                 sh[:, bcx, j * 512:(j + 1) * 512],
                                 start=False, stop=False)
                    t.matmul(PM[j][:, 0:512], buRb[:, dc * 128:(dc + 1) * 128],
                             cmb[:, j * 512:(j + 1) * 512],
                             start=False, stop=True).then_inc(TMS, 1)
            bump("t", 2)
            @block.scalar
            def _(s, dc=dc, t0=t0):
                for j in range(2):
                    s.wait_ge(TMS, t0 + j + 1)
                    s.activation(ffn_ap(dc)[:, j * 512:(j + 1) * 512],
                                 PM[j][:, 0:512], AF.Identity,
                                 bias=bcols[:, B_BSU + dc:B_BSU + dc + 1])
                    s.activation(A.v3("sq9", 1, 1024)[:, 0, j * 512:(j + 1) * 512],
                                 PM[j][:, 0:512], AF.Square,
                                 bias=bcols[:, B_BSU + dc:B_BSU + dc + 1]).then_inc(SE, 1)
            bump("sc", 2)
            sc_e = c["sc"]
            @block.tensor
            def _(t, dc=dc, sc_e=sc_e):
                for j in range(2):
                    t.wait_ge(SE, sc_e - 1 + j)
                    st_sum = PR[0][0:1, 0:512] if j == 0 else PBb[0][0:1, 0:512]
                    st_sq = PR[1][0:1, 0:512] if j == 0 else PBb[1][0:1, 0:512]
                    t.matmul(st_sum, ones_col[:],
                             ffn_ap(dc)[:, j * 512:(j + 1) * 512],
                             start=(dc == 0), stop=(dc == DC - 1), skip_group_check=True)
                    i = t.matmul(st_sq, ones_col[:],
                                 A.v3("sq9", 1, 1024)[:, 0, j * 512:(j + 1) * 512],
                                 start=(dc == 0), stop=(dc == DC - 1),
                                 skip_group_check=True)
                i.then_inc(TMS, 1)
            bump("t", 1)
        t_stats = c["t"]

        A.free("hTa", "hTb", "sh", "wt0", "wt1")
        A.alloc("x1c", DC * 1024)
        @block.sync
        def _(sy):
            sy.wait_ge(TMS, t_stats)
            sy.dma_start(out=A.ap("x1c")[:, 0:DC * 1024], in_=x1_sp[:]).then_inc(SD, 16)
        s_x1 = bump("s") * 16

        for j, abx in ((0, ab_bf), (1, ab2_bf)):
            su = PR[0][0:1, 0:512] if j == 0 else PBb[0][0:1, 0:512]
            sq = PR[1][0:1, 0:512] if j == 0 else PBb[1][0:1, 0:512]
            @block.vector
            def _(v, su=su, sq=sq):
                v.wait_ge(TMS, t_stats)
                v.wait_ge(SE, c["sc"])
                m, ms = strow[0:1, 0:512], strow[0:1, 512:1024]
                var = strow[0:1, 1024:1536]
                v.tensor_single_scalar(m, su, 1.0 / D, AL.mult)
                v.tensor_single_scalar(ms, sq, 1.0 / D, AL.mult)
                v.tensor_tensor(var, m, m, AL.mult)
                v.tensor_sub(var, ms, var).then_inc(VE, 1)
            vf1 = bump("v")
            @block.scalar
            def _(s, vf1=vf1):
                s.wait_ge(VE, vf1)
                s.activation(strow[0:1, 1536:2048], strow[0:1, 1024:1536],
                             AF.Sqrt, bias=eps_col[0:1, 0:1]).then_inc(SE, 1)
            sf1 = bump("sc")
            @block.vector
            def _(v, abx=abx, sf1=sf1):
                v.wait_ge(SE, sf1)
                a, b = strow[0:1, 2048:2560], strow[0:1, 2560:3072]
                v.reciprocal(a, strow[0:1, 1536:2048])
                v.scalar_tensor_tensor(b, strow[0:1, 0:512], -1.0, a, AL.mult, AL.mult)
                v.tensor_copy(abx[0:1, 0:512], a)
                v.tensor_copy(abx[0:1, 512:1024], b).then_inc(VE, 1)
            vf2 = bump("v")
            @block.tensor
            def _(t, abx=abx, vf2=vf2):
                t.wait_ge(VE, vf2)
                t.matmul(PM[2][:, 0:512], ones_row[:], abx[0:1, 0:512],
                         start=True, stop=True)
                t.matmul(PM[3][:, 0:512], ones_row[:], abx[0:1, 512:1024],
                         start=True, stop=True).then_inc(TMS, 1)
            tb = bump("t")
            @block.scalar
            def _(s, tb=tb):
                s.wait_ge(TMS, tb)
                s.copy(ABt[:, 0:512], PM[2][:, 0:512])
                s.copy(ABt[:, 512:1024], PM[3][:, 0:512]).then_inc(SE, 1)
            sb = bump("sc")
            v0 = c["v"]
            s0y = c["s"]
            @block.vector
            def _(v, j=j, sb=sb, s0y=s0y):
                v.wait_ge(SE, sb)
                v.wait_ge(SD, s_x1)
                x13 = A.v3("x1c", DC, 1024)
                for dc in range(DC):
                    if dc >= 2:
                        v.wait_ge(SD, (s0y + dc - 1) * 16)
                    elif j == 1:
                        v.wait_ge(SD, s0y * 16)
                    v.tensor_mul(ugv[:], ffn_ap(dc)[:, j * 512:(j + 1) * 512],
                                 ABt[:, 0:512])
                    v.tensor_add(ugv[:], ugv[:], ABt[:, 512:1024])
                    v.tensor_add(outb[dc % 2][:], ugv[:],
                                 x13[:, dc, j * 512:(j + 1) * 512]).then_inc(VE, 1)
            bump("v", DC)
            @block.sync
            def _(sy, j=j, v0=v0):
                for dc in range(DC):
                    sy.wait_ge(VE, v0 + dc + 1)
                    sy.dma_start(out=y_d[dc * 128:(dc + 1) * 128,
                                         j * 512:(j + 1) * 512],
                                 in_=outb[dc % 2][:]).then_inc(SD, 16)
            bump("s", DC)

        if debug:
            @block.sync
            def _(sy):
                for nm, src in (("wub", wub_sp), ("gsp", g_sp), ("xn", xn_sp), ("gt", gt_sp),
                                ("kc", kc_sp), ("cv", cv_sp), ("x1", x1_sp)):
                    sy.dma_start(out=dbg[nm][:], in_=src[:]).then_inc(SD, 16)
            bump("s", 5)

        barrier()

    return nc, dbg


def _build_wupm(wu, ws_u):
    f32 = np.float32
    wuR = wu.reshape(E, DC, 128, BC, 128).transpose(1, 0, 3, 4, 2)
    a = wuR.transpose(0, 3, 1, 2, 4).reshape(DC, 128, 32 * 128)
    wsuR = ws_u.T.reshape(BC, 128, DC, 128).transpose(2, 0, 1, 3)
    b = wsuR.transpose(0, 2, 1, 3).reshape(DC, 128, 4 * 128)
    return np.ascontiguousarray(np.concatenate([a, b], axis=2), dtype=f32)


def make_inputs(inputs):
    f32 = np.float32
    x = np.ascontiguousarray(np.asarray(inputs["L0_attn_x"], dtype=f32)[0].T)
    Lg = x.shape[1]
    Wu = np.asarray(inputs["L0_attn_Wu"], f32)
    Wv = np.asarray(inputs["L0_attn_Wv"], f32)
    Wg1 = np.asarray(inputs["L0_attn_Wg1"], f32)
    Wg2 = np.asarray(inputs["L0_attn_Wg2"], f32)
    conv_w = np.asarray(inputs["L0_attn_conv_w"], f32)
    gate_w = np.asarray(inputs["L0_ffn_gate_w"], f32)
    wd = np.asarray(inputs["L0_ffn_wd"], f32)
    wu = np.asarray(inputs["L0_ffn_wu"], f32)
    ws_d = np.asarray(inputs["L0_ffn_ws_d"], f32)
    ws_u = np.asarray(inputs["L0_ffn_ws_u"], f32)

    shared = {
        "wuT": np.ascontiguousarray(Wu.T),
        "wvT": np.ascontiguousarray(Wv.T),
        "wg1T": np.ascontiguousarray(Wg1.T),
        "wg2T": np.ascontiguousarray(Wg2.T),
        "gateT": np.ascontiguousarray(gate_w.T),
        "wdT": np.ascontiguousarray(np.concatenate(
            [np.transpose(wd, (0, 2, 1)), ws_d.T[None]], axis=0)),
        "wuPM": _build_wupm(wu, ws_u),
        "buR": np.ascontiguousarray(
            np.asarray(inputs["L0_ffn_bu"], f32).reshape(E, DC, 128).transpose(1, 0, 2)),
    }
    brows = np.zeros((NBR, 128), f32)
    brows[B_BU:B_BU + 16] = np.asarray(inputs["L0_attn_bu"], f32).reshape(16, 128)
    brows[B_BV:B_BV + 16] = np.asarray(inputs["L0_attn_bv"], f32).reshape(16, 128)
    brows[B_BG2:B_BG2 + 16] = np.asarray(inputs["L0_attn_bg2"], f32).reshape(16, 128)
    brows[B_BG1:B_BG1 + 4] = np.asarray(inputs["L0_attn_bg1"], f32).reshape(4, 128)
    brows[B_CB:B_CB + 16] = np.asarray(inputs["L0_attn_conv_b"], f32).reshape(16, 128)
    brows[B_BSD:B_BSD + 4] = np.asarray(inputs["L0_ffn_bs_d"], f32).reshape(4, 128)
    brows[B_BSU:B_BSU + 16] = np.asarray(inputs["L0_ffn_bs_u"], f32).reshape(16, 128)
    brows[B_BD:B_BD + 32] = np.asarray(inputs["L0_ffn_bd"], f32).reshape(32, 128)
    for k in range(3):
        brows[B_CW + 16 * k:B_CW + 16 * (k + 1)] = conv_w[:, 0, k].reshape(16, 128)
    shared["brows"] = brows
    oh = np.zeros((8, E * 128), f32)
    for e in range(E):
        oh[e, e * 128:(e + 1) * 128] = 1.0
    shared["onehots"] = oh

    in_maps = []
    for core in range(8):
        p, q = core // 2, core & 1
        s = 2048 * p + 1024 * q
        xm = np.zeros((D, TM + 2), f32)
        lo, hi = s - 1, s + TM + 1
        a, b = max(lo, 0), min(hi, Lg)
        xm[:, a - lo:a - lo + (b - a)] = x[:, a:b]
        pl = 1024 * p
        xp = np.zeros((D, TP), f32)
        if pl:
            xp[:, TP - pl:] = x[:, q * pl:(q + 1) * pl]
        npc = np.zeros((128, 2), f32)
        npc[:, 0] = -(TP - pl)
        npc[:, 1] = float(q)
        m = dict(shared)
        m["xm"] = np.ascontiguousarray(
            xm.reshape(DC, 128, TM + 2).transpose(1, 0, 2))
        m["xp"] = np.ascontiguousarray(
            xp.reshape(DC, 128, 6, 512).transpose(2, 1, 0, 3).reshape(6, 128, DC * 512))
        m["npcols"] = npc
        in_maps.append(m)
    return in_maps


def kernel(**inputs):
    rng = np.random.default_rng()
    nonce0 = int(rng.integers(1, 2**31 - 1))
    nonce1 = int(rng.integers(1, 2**31 - 1))
    in_maps = make_inputs(inputs)
    nc, _ = build_program(nonce0, nonce1, debug=False)
    res = run_bass_kernel_spmd(nc, in_maps, core_ids=list(range(8)))
    out = np.empty((1, 8192, D), np.float32)
    for core in range(8):
        p, q = core // 2, core & 1
        s = 2048 * p + 1024 * q
        out[0, s:s + TM, :] = np.asarray(res.results[core]["y"]).T
    return out
